# version 11 (final)
# baseline (speedup 1.0000x reference)
import numpy as np
import ml_dtypes
import concourse.bass as bass
import concourse.mybir as mybir
from concourse.bass_utils import run_bass_kernel_spmd

F32 = mybir.dt.float32
BF16 = mybir.dt.bfloat16
AF = mybir.ActivationFunctionType
ALU = mybir.AluOpType
AXX = mybir.AxisListType.X

NCORES = 8
D = 1024
T = 2048
NCH = 8
NT = 4
TAU = 512
H = 16
HD = 64
FP = 256
NE = 10
EPS_LN = 1e-5
EPS_ATTN = 1e-8

ENGS = ("tensor", "vector", "scalar", "gpsimd", "sync")
DQUEUES = {"gdma": "gpsimd", "sdma": "sync"}


class Prog:

    def __init__(self):
        self.items = []
        self.nidx = {k: 0 for k in list(ENGS) + list(DQUEUES)}
        self.last_w = {}
        self.readers = {}
        self.region = {}
        self.arena_keys = {}
        self.forced = set()

    def reg(self, key, arena, lo, hi):
        self.region[key] = (arena, lo, hi)
        self.arena_keys.setdefault(arena, []).append(key)

    def _overlaps(self, key):
        r = self.region.get(key)
        if r is None:
            return ()
        a, lo, hi = r
        out = []
        for k2 in self.arena_keys[a]:
            if k2 == key:
                continue
            _, lo2, hi2 = self.region[k2]
            if lo < hi2 and lo2 < hi:
                out.append(k2)
        return out

    def _deps(self, reads, writes, scalar_reads):
        deps = {}

        def dep_read(r, force):
            t = self.last_w.get(r)
            if t is not None:
                if force or t in self.forced:
                    deps[t] = True
                else:
                    deps.setdefault(t, False)

        def dep_write(w):
            t = self.last_w.get(w)
            if t is not None:
                if t in self.forced:
                    deps[t] = True
                else:
                    deps.setdefault(t, False)
            for t2 in self.readers.get(w, ()):
                deps.setdefault(t2, False)

        for r in reads:
            dep_read(r, False)
            for k2 in self._overlaps(r):
                dep_read(k2, False)
        for r in scalar_reads:
            dep_read(r, True)
            for k2 in self._overlaps(r):
                dep_read(k2, True)
        for w in writes:
            dep_write(w)
            for k2 in self._overlaps(w):
                dep_write(k2)
        return deps

    def _post(self, tok, reads, writes, scalar_reads):
        for w in writes:
            self.last_w[w] = tok
            self.readers[w] = []
        for r in list(reads) + list(scalar_reads):
            self.readers.setdefault(r, []).append(tok)

    def op(self, eng, fn, reads=(), writes=(), scalar_reads=(), sync_out=False):
        assert eng in ENGS
        deps = self._deps(reads, writes, scalar_reads)
        for t in list(deps):
            if t[0] in ("vector", "scalar") or (t[0] == eng and eng != "tensor"):
                deps[t] = True
        idx = self.nidx[eng]
        self.nidx[eng] += 1
        tok = (eng, idx)
        self.items.append(dict(kind="op", eng=eng, fn=fn, idx=idx, deps=deps))
        self._post(tok, reads, writes, scalar_reads)
        if sync_out:
            self.forced.add(tok)
        return tok

    SERIALIZE_DMA = True

    def dma(self, queue, fn, reads=(), writes=()):
        assert queue in DQUEUES
        deps = self._deps(reads, writes, ())
        if self.SERIALIZE_DMA and self.nidx[queue] > 0:
            deps.setdefault((queue, self.nidx[queue] - 1), False)
        idx = self.nidx[queue]
        self.nidx[queue] += 1
        tok = (queue, idx)
        self.items.append(dict(kind="dma", eng=DQUEUES[queue], queue=queue,
                               fn=fn, idx=idx, deps=deps))
        self._post(tok, reads, writes, ())
        return tok

    def finalize(self, sems, dma_sems):
        marked = {e: set() for e in ENGS}
        for it in self.items:
            for (pe, pidx), force in it["deps"].items():
                if pe in ENGS:
                    if pe == it["eng"] and not force:
                        continue
                    marked[pe].add(pidx)
        rank = {}
        for e in ENGS:
            for i, idx in enumerate(sorted(marked[e])):
                rank[(e, idx)] = i + 1
        per_eng = {e: [] for e in ENGS}
        for it in self.items:
            per_eng[it["eng"]].append(it)

        def replay(ename):
            def body(engobj):
                hwm = {}
                for it in per_eng[ename]:
                    for (pe, pidx), force in sorted(it["deps"].items()):
                        if pe in ENGS:
                            if pe == ename and not force:
                                continue
                            sem, val = sems[pe], rank[(pe, pidx)]
                        else:
                            sem, val = dma_sems[pe], 16 * (pidx + 1)
                        if hwm.get(sem.name, 0) < val:
                            engobj.wait_ge(sem, val)
                            hwm[sem.name] = val
                    inst = it["fn"](engobj)
                    if it["kind"] == "dma":
                        inst.then_inc(dma_sems[it["queue"]], 16)
                    elif it["idx"] in marked[ename]:
                        inst.then_inc(sems[ename], 1)
            return body

        return replay


def build_program(stop_after=None):
    nc = bass.Bass("TRN2", target_bir_lowering=False)

    d_x = nc.dram_tensor("x", [D, T], F32, kind="ExternalInput").ap()
    d_wqkvh = nc.dram_tensor("wqkvh", [H, D, 192], F32, kind="ExternalInput").ap()
    d_proj = nc.dram_tensor("proj", [H, HD, FP], F32, kind="ExternalInput").ap()
    d_woutT = nc.dram_tensor("woutT", [D, D], F32, kind="ExternalInput").ap()
    d_w1T = nc.dram_tensor("w1T", [D, 4 * D], F32, kind="ExternalInput").ap()
    d_w2T = nc.dram_tensor("w2T", [4 * D, D], F32, kind="ExternalInput").ap()
    d_wrT = nc.dram_tensor("wrT", [D, 8], F32, kind="ExternalInput").ap()
    d_w1all = nc.dram_tensor("w1all", [NE, D, D], F32, kind="ExternalInput").ap()
    d_w2all = nc.dram_tensor("w2all", [NE, D, D], F32, kind="ExternalInput").ap()
    d_boutc = nc.dram_tensor("boutc", [128, 8], F32, kind="ExternalInput").ap()
    d_b2c = nc.dram_tensor("b2c", [128, 8], F32, kind="ExternalInput").ap()
    d_b1c = nc.dram_tensor("b1c", [128, 32], F32, kind="ExternalInput").ap()
    d_gz = nc.dram_tensor("gz", [2, 2 * D], F32, kind="ExternalInput").ap()
    d_beta = nc.dram_tensor("betar", [1, D], F32, kind="ExternalInput").ap()
    d_sel = nc.dram_tensor("selrow", [2, 256], F32, kind="ExternalInput").ap()
    d_oneh = nc.dram_tensor("oneh", [8, D], F32, kind="ExternalInput").ap()
    d_idf = nc.dram_tensor("idf", [128, 128], F32, kind="ExternalInput").ap()
    d_csts = nc.dram_tensor("csts", [128, 8], F32, kind="ExternalInput").ap()
    d_out = nc.dram_tensor("o", [D, T], F32, kind="ExternalOutput").ap()

    X = [nc.alloc_sbuf_tensor(f"Xc{c}", [128, T], F32) for c in range(NCH)]
    Y = [nc.alloc_sbuf_tensor(f"Yc{c}", [128, T], BF16) for c in range(NCH)]
    U = nc.alloc_sbuf_tensor("U", [128, 73728], mybir.dt.uint8)
    A2 = nc.alloc_sbuf_tensor("A2", [128, 32768], mybir.dt.uint8)
    ubase = nc.lookup_mloc(U).addr
    abase = nc.lookup_mloc(A2).addr

    p = Prog()

    def at(name, shape, dtype, base, off, key, arena, nbytes):
        t = nc.alloc_sbuf_tensor_at(name, shape, dtype, offset=base + off)
        if isinstance(key, str):
            p.reg(key, arena, off, off + nbytes)
        return t

    u_w1e = [at(f"u_w1e{i}", [128, 8, D], BF16, ubase, 16384 * i,
                f"w1e{i}", "U", 16384) for i in range(2)]
    u_w2e = [at(f"u_w2e{i}", [128, 8, D], BF16, ubase, 32768 + 16384 * i,
                f"w2e{i}", "U", 16384) for i in range(2)]
    u_hF = at("u_hF", [128, 4, TAU], BF16, ubase, 65536, None, "U", 0)
    for i in range(4):
        p.reg(f"hF{i}", "U", 65536 + 1024 * i, 65536 + 1024 * (i + 1))
    u_att = at("u_att", [128, 8, T], BF16, ubase, 0, "att", "U", 32768)
    u_qh = at("u_qh", [64, T], BF16, ubase, 32768, "qh", "U", 4096)
    u_kh = at("u_kh", [64, T], BF16, ubase, 36864, "kh", "U", 4096)
    u_vh = at("u_vh", [64, T], BF16, ubase, 40960, "vh", "U", 4096)
    u_qf = at("u_qf", [128, 2, T], BF16, ubase, 45056, "qf", "U", 8192)
    u_kfm = at("u_kfm", [128, 2, T], BF16, ubase, 53248, "kfm", "U", 8192)
    u_kft = at("u_kft", [128, 16, 128], BF16, ubase, 32768, "kft", "U", 4096)
    u_attst = at("u_attst", [64, T], BF16, ubase, 36864, "attst", "U", 4096)
    u_vt = at("u_vt", [128, 16, 65], BF16, ubase, 61440, "vt", "U", 2080)
    p.reg("vt_ones", "U", 61440, 61440 + 2080)
    u_qkvw = [at(f"u_qkvw{i}", [128, 8, 192], BF16, ubase, 63552 + 3072 * i,
                 f"qkvw{i}", "U", 3072) for i in range(2)]
    u_y3 = at("u_y3", [128, 8, T], BF16, ubase, 0, None, "U", 0)
    for c in range(NCH):
        p.reg(f"y3{c}", "U", 4096 * c, 4096 * (c + 1))
    p.reg("lnscr", "U", 0, 4096)
    u_acc = at("u_acc", [128, 8, TAU], F32, ubase, 32768, "acc", "U", 16384)
    u_mw1 = [at(f"u_mw1{i}", [128, 8, 512], BF16, ubase, 49152 + 8192 * i,
                f"mw1{i}", "U", 8192) for i in range(2)]
    u_mw2 = at("u_mw2", [128, 4, D], BF16, ubase, 65536, "mw2", "U", 8192)

    a_st = [at(f"a_st{i}", [128, 2, 1024], BF16, abase, 4096 * i,
               f"ast{i}", "A2", 4096) for i in range(2)]
    p.reg("ast0sq", "A2", 2048, 4096)
    p.reg("ast1sq", "A2", 6144, 8192)
    a_rM = at("a_rM", [1, 512], F32, abase, 8192, "rM", "A2", 2048)
    a_rV = at("a_rV", [1, 512], F32, abase, 10240, "rV", "A2", 2048)
    a_rS = at("a_rS", [1, 512], F32, abase, 12288, "rS", "A2", 2048)
    a_rbf = at("a_rbf", [2, T], BF16, abase, 24576, "rbf", "A2", 4096)
    a_r2b = at("a_r2b", [1, T], BF16, abase, 14336, "r2b", "A2", 4096)
    a_gm = at("a_gm", [128, 16, TAU], BF16, abase, 0, None, "A2", 0)
    for i in range(16):
        p.reg(f"gm{i}", "A2", 1024 * i, 1024 * (i + 1))
    a_lg = at("a_lg", [9, T], F32, abase, 8192, "lg", "A2", 8192)
    a_ltok = at("a_ltok", [128, 144], F32, abase, 16384, "ltok", "A2", 576)
    a_m8 = at("a_m8", [128, 128], F32, abase, 16960, "m8", "A2", 512)
    a_sel = at("a_sel", [128, 128], F32, abase, 17472, "sel", "A2", 512)
    a_wrk = at("a_wrk", [128, 128], F32, abase, 17984, "wrk", "A2", 512)
    a_nm = at("a_nm", [128, 16], F32, abase, 18496, "nm", "A2", 64)
    a_dn = at("a_dn", [128, 16], F32, abase, 18560, "dn", "A2", 64)
    a_rc = at("a_rc", [128, 16], F32, abase, 18624, "rc", "A2", 64)
    a_mu = at("a_mu", [1, T], BF16, abase, 19968, "mu", "A2", 4096)
    a_mrow = at("a_mrow", [8, T], BF16, abase, 19456, "mrow", "A2", 4096)
    a_mb = [at(f"a_mb{i}", [128, T], BF16, abase, 23552 + 4096 * i,
               f"mb{i}", "A2", 4096) for i in range(2)]
    a_elu = [at(f"a_elu{i}", [128, TAU], BF16, abase, 1024 * i,
                f"elu{i}", "A2", 1024) for i in range(2)]
    a_elu2 = [at(f"a_elu2{i}", [128, TAU], BF16, abase, 7168 + 1024 * i,
                 f"elu2{i}", "A2", 1024) for i in range(2)]
    a_zb = at("a_zb", [64, TAU], BF16, abase, 2048, "zb", "A2", 1024)
    a_z = at("a_z", [1, T], BF16, abase, 3072, "z", "A2", 4096)
    a_gz = at("a_gz", [2, 2 * D], BF16, abase, 18432, "gz", "A2", 4096)
    a_beta = at("a_beta", [1, D], BF16, abase, 22528, "beta", "A2", 2048)
    a_wout = at("a_wout", [128, 8, D], BF16, abase, 14336, "wout", "A2", 16384)
    a_kvks = at("a_kvks", [128, 2, 65], BF16, abase, 31744, "kvks", "A2", 260)
    a_hM = at("a_hM", [128, 4, TAU], BF16, abase, 0, None, "A2", 0)
    for i in range(4):
        p.reg(f"hM{i}", "A2", 1024 * i, 1024 * (i + 1))
    a_hM2 = at("a_hM2", [128, 4, TAU], BF16, abase, 4096, None, "A2", 0)
    for i in range(4):
        p.reg(f"hN{i}", "A2", 4096 + 1024 * i, 4096 + 1024 * (i + 1))
    a_ph = [at(f"a_ph{i}", [64, FP], BF16, abase, 30720 + 512 * i,
               f"ph{i}", "A2", 512) for i in range(2)]

    s_id32 = nc.alloc_sbuf_tensor("s_id32", [128, 128], F32)
    s_idbf = nc.alloc_sbuf_tensor("s_idbf", [128, 128], BF16)
    s_ones = nc.alloc_sbuf_tensor("s_ones", [1, 512], BF16)
    s_onec = nc.alloc_sbuf_tensor("s_onec", [128, 1], BF16)
    s_sel = nc.alloc_sbuf_tensor("s_sel", [2, 256], BF16)
    s_oneh = nc.alloc_sbuf_tensor("s_oneh", [8, D], BF16)
    s_wro = nc.alloc_sbuf_tensor("s_wro", [128, 8, 8], BF16)
    s_wro32 = nc.alloc_sbuf_tensor("s_wro32", [128, 8, 8], F32)
    s_onec32 = nc.alloc_sbuf_tensor("s_onec32", [128, 1], F32)
    s_csrb = nc.alloc_sbuf_tensor("s_csrb", [1, 8], BF16)
    s_bout = nc.alloc_sbuf_tensor("s_bout", [128, 8], F32)
    s_b2 = nc.alloc_sbuf_tensor("s_b2", [128, 8], F32)
    s_b1 = nc.alloc_sbuf_tensor("s_b1", [128, 32], F32)
    s_cst = nc.alloc_sbuf_tensor("s_cst", [128, 8], F32)

    PS = [nc.alloc_psum_tensor(f"PS{i}", [128, 512], F32) for i in range(7)]
    PT = nc.alloc_psum_tensor("PT", [128, 1024], BF16)

    C_EPS, C_NHALF, C_SQH, C_NEG1 = 0, 1, 2, 3

    def ffn_load_e(e):
        s = e % 2
        p.dma("gdma", (lambda en, e=e, s=s: en.dma_start(
            out=u_w1e[s][:], in_=d_w1all[e].rearrange("(k p) m -> p k m", p=128))),
            writes=(f"w1e{s}",))
        p.dma("gdma", (lambda en, e=e, s=s: en.dma_start(
            out=u_w2e[s][:], in_=d_w2all[e].rearrange("(k p) m -> p k m", p=128))),
            writes=(f"w2e{s}",))

    for c in range(NCH):
        p.dma("sdma", (lambda e, c=c: e.dma_start(
            out=X[c][:], in_=d_x[c * 128:(c + 1) * 128, :])), writes=(f"X{c}",))
    p.dma("gdma", lambda e: e.dma_start(out=s_id32[:], in_=d_idf), writes=("id32",))
    p.dma("gdma", lambda e: e.dma_start(out=s_idbf[:], in_=d_idf), writes=("idbf",))
    p.dma("gdma", lambda e: e.dma_start(out=s_sel[:], in_=d_sel), writes=("selr",))
    p.dma("gdma", lambda e: e.dma_start(out=s_oneh[:], in_=d_oneh), writes=("oneh",))
    p.dma("gdma", lambda e: e.dma_start(
        out=s_wro[:], in_=d_wrT.rearrange("(k p) m -> p k m", p=128)), writes=("wro",))
    p.dma("gdma", lambda e: e.dma_start(
        out=s_wro32[:], in_=d_wrT.rearrange("(k p) m -> p k m", p=128)),
        writes=("wro32",))
    p.dma("gdma", lambda e: e.dma_start(out=s_bout[:], in_=d_boutc), writes=("bout",))
    p.dma("gdma", lambda e: e.dma_start(out=s_b2[:], in_=d_b2c), writes=("b2",))
    p.dma("gdma", lambda e: e.dma_start(out=s_b1[:], in_=d_b1c), writes=("b1",))
    p.dma("gdma", lambda e: e.dma_start(out=s_cst[:], in_=d_csts), writes=("cst",))
    ffn_load_e(0)
    ffn_load_e(1)
    p.op("vector", lambda e: e.memset(s_ones[:], 1.0), writes=("onesr",))
    p.op("vector", lambda e: e.memset(s_onec[:], 1.0), writes=("onec",))
    p.op("vector", lambda e: e.memset(s_onec32[:], 1.0), writes=("onec32",))

    psrot = [0]

    def take_ps():
        i = psrot[0] % 7
        psrot[0] += 1
        return PS[i], f"PS{i}"

    def emit_ln(src_aps, src_keys, src_is_bf, gamma_beta, dst_aps, dst_keys,
                add_into_x=False, dbg_rows=False):
        for half in range(2):
            h0 = half * 1024
            pstiles = [take_ps() for _ in range(4)]
            for c in range(NCH):
                st = a_st[c % 2]
                stk = f"ast{c % 2}"
                sqk = f"ast{c % 2}sq"
                if src_is_bf:
                    xb_ap = src_aps[c][:, h0:h0 + 1024]
                    xb_key = src_keys[c]
                else:
                    p.op("vector",
                         (lambda e, c=c, h0=h0, st=st: e.tensor_copy(
                             st[:, 0, :], src_aps[c][:, h0:h0 + 1024])),
                         reads=(src_keys[c],), writes=(stk,))
                    xb_ap = st[:, 0, :]
                    xb_key = stk
                p.op("vector",
                     (lambda e, st=st, xb_ap=xb_ap: e.tensor_tensor(
                         st[:, 1, :], xb_ap, xb_ap, ALU.mult)),
                     reads=(xb_key,), writes=(sqk,))
                for tt in range(2):
                    for which in range(2):
                        pst, pstk = pstiles[2 * which + tt]
                        rhs = xb_ap[:, tt * 512:(tt + 1) * 512] if which == 0 \
                            else st[:, 1, tt * 512:(tt + 1) * 512]
                        rk = xb_key if which == 0 else sqk
                        p.op("tensor",
                             (lambda e, pst=pst, rhs=rhs, c=c: e.matmul(
                                 pst[0:1, :], s_onec[:], rhs,
                                 start=(c == 0), stop=(c == NCH - 1))),
                             reads=(rk, "onec"), writes=(pstk,))
            inv_n = 1.0 / D
            for tt in range(2):
                col = h0 + tt * 512
                ps1, k1 = pstiles[tt]
                ps2, k2 = pstiles[2 + tt]
                p.op("vector", (lambda e, ps1=ps1: e.tensor_scalar_mul(
                    a_rM[0:1, :], ps1[0:1, :], inv_n)), reads=(k1,), writes=("rM",))
                p.op("vector", (lambda e, ps2=ps2: e.tensor_scalar_mul(
                    a_rV[0:1, :], ps2[0:1, :], inv_n)), reads=(k2,), writes=("rV",))
                if dbg_rows and half == 0 and tt == 0:
                    p.dma("gdma", (lambda e: e.dma_start(
                        out=d_out[7:8, 0:512], in_=a_rV[0:1, :])),
                        reads=("rV",), writes=("dbg3",))
                    p.op("vector", (lambda e, ps2=ps2: e.tensor_copy(
                        a_wrk[:, 0:4], ps2[0:128, 0:4])), reads=(k2,),
                        writes=("wrk",))
                    p.dma("gdma", (lambda e: e.dma_start(
                        out=d_out[9:10, 0:4], in_=a_wrk[0:1, 0:4])),
                        reads=("wrk",), writes=("dbg5",))
                    p.dma("gdma", (lambda e: e.dma_start(
                        out=d_out[10:138, 0:1024],
                        in_=a_st[1][:, 1, :])),
                        reads=("ast1sq",), writes=("dbg6",))
                    p.dma("gdma", (lambda e: e.dma_start(
                        out=d_out[138:266, 0:1024],
                        in_=a_st[0][:, 1, :])),
                        reads=("ast0sq",), writes=("dbg7",))
                p.op("scalar", (lambda e: e.activation(
                    a_rS[0:1, :], a_rM[0:1, :], AF.Square)),
                    reads=("rM",), writes=("rS",))
                p.op("vector", (lambda e: e.tensor_tensor(
                    a_rV[0:1, :], a_rV[0:1, :], a_rS[0:1, :], ALU.subtract)),
                    reads=("rV", "rS"), writes=("rV",))
                if dbg_rows and half == 0 and tt == 0:
                    p.dma("gdma", (lambda e: e.dma_start(
                        out=d_out[8:9, 0:512], in_=a_rV[0:1, :])),
                        reads=("rV",), writes=("dbg4",))
                p.op("scalar", (lambda e: e.activation(
                    a_rV[0:1, :], a_rV[0:1, :], AF.Ln,
                    bias=s_cst[0:1, C_EPS:C_EPS + 1])),
                    reads=("rV", "cst"), writes=("rV",))
                p.op("scalar", (lambda e, col=col: e.activation(
                    a_rbf[0:1, col:col + 512], a_rV[0:1, :], AF.Exp,
                    scale=s_cst[0:1, C_NHALF:C_NHALF + 1])),
                    reads=("rV", "cst"), writes=("rbf",))
                p.op("vector", (lambda e, col=col: e.scalar_tensor_tensor(
                    a_r2b[0:1, col:col + 512], a_rM[0:1, :], -1.0,
                    a_rbf[0:1, col:col + 512], ALU.mult, ALU.mult)),
                    reads=("rM", "rbf"), writes=("r2b",))
                if not gamma_beta:
                    p.op("vector", (lambda e, col=col: e.tensor_copy(
                        a_mu[0:1, col:col + 512], a_rM[0:1, :])),
                        reads=("rM",), writes=("mu",))
                if dbg_rows and half == 0 and tt == 0:
                    p.dma("gdma", (lambda e: e.dma_start(
                        out=d_out[4:5, 0:512], in_=a_rM[0:1, :])),
                        reads=("rM",), writes=("dbg0",))
                    p.dma("gdma", (lambda e: e.dma_start(
                        out=d_out[5:6, 0:512], in_=a_rV[0:1, :])),
                        reads=("rV",), writes=("dbg1",))
                    p.dma("gdma", (lambda e: e.dma_start(
                        out=d_out[6:7, 0:512], in_=a_rS[0:1, :])),
                        reads=("rS",), writes=("dbg2",))
        p.dma("gdma", lambda e: e.dma_start(out=a_rbf[1:2, :], in_=a_r2b[0:1, :]),
              reads=("r2b",), writes=("rbf",))
        for c in range(NCH):
            for t in range(NT):
                tsl = slice(t * TAU, (t + 1) * TAU)
                psA, kA = take_ps()
                psB, kB = take_ps()
                if gamma_beta:
                    p.op("tensor", (lambda e, psA=psA, tsl=tsl, c=c: e.matmul(
                        psA[:], a_gz[:, c * 128:(c + 1) * 128],
                        a_rbf[:, tsl], start=True, stop=True)),
                        reads=("gz", "rbf"), writes=(kA,))
                    p.op("tensor", (lambda e, psB=psB, tsl=tsl, c=c: e.matmul(
                        psB[:], a_gz[:, D + c * 128:D + (c + 1) * 128],
                        a_rbf[:, tsl], start=True, stop=False)),
                        reads=("gz", "rbf"), writes=(kB,))
                    p.op("tensor", (lambda e, psB=psB, c=c: e.matmul(
                        psB[:], a_beta[0:1, c * 128:(c + 1) * 128],
                        s_ones[0:1, :], start=False, stop=True)),
                        reads=("beta", "onesr"), writes=(kB,))
                else:
                    p.op("tensor", (lambda e, psA=psA, tsl=tsl: e.matmul(
                        psA[:], s_sel[:, 0:128], a_rbf[:, tsl],
                        start=True, stop=True)),
                        reads=("selr", "rbf"), writes=(kA,))
                    p.op("tensor", (lambda e, psB=psB, tsl=tsl: e.matmul(
                        psB[:], s_sel[:, 128:256], a_rbf[:, tsl],
                        start=True, stop=True)),
                        reads=("selr", "rbf"), writes=(kB,))
                if not add_into_x:
                    p.op("vector", (lambda e, c=c, tsl=tsl, psA=psA: e.tensor_tensor(
                        dst_aps[c][:, tsl], src_aps[c][:, tsl], psA[:], ALU.mult)),
                        reads=(src_keys[c], kA), writes=(dst_keys[c],))
                    p.op("vector", (lambda e, c=c, tsl=tsl, psB=psB: e.tensor_tensor(
                        dst_aps[c][:, tsl], dst_aps[c][:, tsl], psB[:], ALU.add)),
                        reads=(dst_keys[c], kB), writes=(dst_keys[c],))
                else:
                    p.op("vector", (lambda e, c=c, tsl=tsl, psA=psA: e.tensor_tensor(
                        dst_aps[0][:, tsl], src_aps[c][:, tsl], psA[:], ALU.mult)),
                        reads=(src_keys[c], kA), writes=(dst_keys[0],))
                    p.op("vector", (lambda e, tsl=tsl, psB=psB: e.tensor_tensor(
                        dst_aps[0][:, tsl], dst_aps[0][:, tsl], psB[:], ALU.add)),
                        reads=(dst_keys[0], kB), writes=(dst_keys[0],))
                    p.op("vector", (lambda e, c=c, tsl=tsl: e.tensor_tensor(
                        X[c][:, tsl], X[c][:, tsl], dst_aps[0][:, tsl], ALU.add)),
                        reads=(f"X{c}", dst_keys[0]), writes=(f"X{c}",))

    def emit_routing():
        psc, kc = take_ps()
        for c in range(NCH):
            p.op("tensor", (lambda e, psc=psc, c=c: e.matmul(
                psc[0:1, 0:8], s_onec32[:], s_wro32[:, c, :],
                start=(c == 0), stop=(c == NCH - 1))),
                reads=("onec32", "wro32"), writes=(kc,))
        p.op("vector", (lambda e, psc=psc: e.tensor_copy(
            s_csrb[:], psc[0:1, 0:8])), reads=(kc,), writes=("csr",))
        for t in range(NT):
            tsl = slice(t * TAU, (t + 1) * TAU)
            ps, k = take_ps()
            for c in range(NCH):
                p.op("tensor", (lambda e, ps=ps, c=c, tsl=tsl: e.matmul(
                    ps[0:8, :], s_wro32[:, c, :], X[c][:, tsl],
                    start=(c == 0), stop=(c == NCH - 1))),
                    reads=("wro32", f"X{c}"), writes=(k,))
            p.op("vector", (lambda e, ps=ps, tsl=tsl: e.tensor_copy(
                a_lg[0:8, tsl], ps[0:8, :])), reads=(k,), writes=("lg",))
            ps2, k2 = take_ps()
            p.op("tensor", (lambda e, ps2=ps2, tsl=tsl: e.matmul(
                ps2[0:8, :], s_csrb[:], a_mu[0:1, tsl],
                start=True, stop=True)), reads=("csr", "mu"), writes=(k2,))
            p.op("vector", (lambda e, ps2=ps2, tsl=tsl: e.tensor_tensor(
                a_lg[0:8, tsl], a_lg[0:8, tsl], ps2[0:8, :], ALU.subtract)),
                reads=("lg", k2), writes=("lg",))
        p.dma("gdma", lambda e: e.dma_start(
            out=a_lg[8:9, :], in_=a_rbf[0:1, :]),
            reads=("rbf",), writes=("lg",))
        ps, k = take_ps()
        for j in range(16):
            p.op("tensor", (lambda e, ps=ps, j=j: e.transpose(
                ps[:, 9 * j:9 * j + 9], a_lg[:, 128 * j:128 * (j + 1)],
                s_id32[0:9, 0:9])), reads=("lg", "id32"), writes=(k,))
        p.op("vector", (lambda e, ps=ps: e.tensor_copy(a_ltok[:], ps[:, 0:144])),
             reads=(k,), writes=("ltok",))
        for j in range(16):
            js = slice(9 * j, 9 * j + 8)
            rsl = slice(9 * j + 8, 9 * j + 9)
            p.op("vector", (lambda e, js=js: e.max(out=a_m8[:, 8 * 0:8]
                 if False else a_m8[:, js.start // 9 * 8:js.start // 9 * 8 + 8],
                 in_=a_ltok[:, js])),
                 reads=("ltok",), writes=("m8",), sync_out=True)
            m8s = slice(8 * j, 8 * j + 8)
            p.op("vector", (lambda e, j=j, rsl=rsl: e.scalar_tensor_tensor(
                a_nm[:, j:j + 1], a_m8[:, 8 * j:8 * j + 1], -1.0,
                a_ltok[:, rsl], ALU.mult, ALU.mult)),
                reads=("m8", "ltok"), writes=("nm",))
            p.op("vector", (lambda e, j=j, js=js, m8s=m8s: e.tensor_single_scalar(
                a_sel[:, m8s], a_ltok[:, js], a_m8[:, 8 * j + 1:8 * j + 2],
                ALU.is_ge)),
                reads=("ltok",), writes=("sel",), scalar_reads=("m8",))
            p.op("scalar", (lambda e, j=j, js=js, m8s=m8s, rsl=rsl: e.activation(
                a_wrk[:, m8s], a_ltok[:, js], AF.Exp, bias=a_nm[:, j:j + 1],
                scale=a_ltok[:, rsl])),
                reads=("ltok", "nm"), writes=("wrk",))
            p.op("vector", (lambda e, m8s=m8s: e.tensor_tensor(
                a_wrk[:, m8s], a_wrk[:, m8s], a_sel[:, m8s], ALU.mult)),
                reads=("wrk", "sel"), writes=("wrk",))
            p.op("vector", (lambda e, j=j, m8s=m8s: e.reduce_sum(
                a_dn[:, j:j + 1], a_wrk[:, m8s], axis=AXX)),
                reads=("wrk",), writes=("dn",), sync_out=True)
            p.op("vector", (lambda e, j=j: e.reciprocal(
                a_rc[:, j:j + 1], a_dn[:, j:j + 1])),
                reads=("dn",), writes=("rc",), sync_out=True)
            p.op("vector", (lambda e, j=j, m8s=m8s: e.tensor_scalar(
                a_wrk[:, m8s], a_wrk[:, m8s], a_rc[:, j:j + 1], 0.5,
                ALU.mult, ALU.mult)),
                reads=("wrk",), writes=("wrk",), scalar_reads=("rc",))
        for q in range(4):
            ps2, k2 = take_ps()
            for jj in range(4):
                j = 4 * q + jj
                p.op("tensor", (lambda e, ps2=ps2, j=j, jj=jj: e.transpose(
                    ps2[0:8, 128 * jj:128 * (jj + 1)],
                    a_wrk[:, 8 * j:8 * j + 8], s_id32[:])),
                    reads=("wrk", "id32"), writes=(k2,))
            p.op("vector", (lambda e, ps2=ps2, q=q: e.tensor_copy(
                a_mrow[:, 512 * q:512 * (q + 1)], ps2[0:8, 0:512])),
                reads=(k2,), writes=("mrow",))

    def emit_ffn(next_phase_loads=None, pairs=(0, 1, 2, 3, 4)):
        emit_routing()
        for pair in pairs:
            e0, e1 = 2 * pair, 2 * pair + 1
            for sub, e in ((0, e0), (1, e1)):
                if e < 8:
                    for t in range(NT):
                        tsl = slice(t * TAU, (t + 1) * TAU)
                        ps, k = take_ps()
                        p.op("tensor", (lambda en, ps=ps, e=e, tsl=tsl: en.matmul(
                            ps[:], s_oneh[:, 128 * e:128 * (e + 1)],
                            a_mrow[:, tsl], start=True, stop=True)),
                            reads=("oneh", "mrow"), writes=(k,))
                        p.op("vector", (lambda en, ps=ps, sub=sub, tsl=tsl:
                                        en.tensor_copy(a_mb[sub][:, tsl], ps[:])),
                             reads=(k,), writes=(f"mb{sub}",))
            for t in range(NT):
                tsl = slice(t * TAU, (t + 1) * TAU)
                for sub, e in ((0, e0), (1, e1)):
                    b = e % 2
                    for fc in range(8):
                        gmk = f"gm{sub * 8 + fc}"
                        hslot = fc % 4
                        hk = f"hF{hslot}"
                        ps, k = take_ps()
                        for kk in range(NCH):
                            p.op("tensor", (lambda en, ps=ps, b=b, kk=kk, fc=fc,
                                            tsl=tsl: en.matmul(
                                ps[:], u_w1e[b][:, kk, fc * 128:(fc + 1) * 128],
                                Y[kk][:, tsl], start=(kk == 0), stop=(kk == NCH - 1))),
                                reads=(f"w1e{b}", f"Y{kk}"), writes=(k,))
                        if e < 8:
                            p.op("scalar", (lambda en, ps=ps, hslot=hslot: en.activation(
                                u_hF[:, hslot, :], ps[:], AF.Relu)),
                                reads=(k,), writes=(hk,))
                            p.op("vector", (lambda en, sub=sub, fc=fc, hslot=hslot,
                                            tsl=tsl: en.tensor_tensor(
                                a_gm[:, sub * 8 + fc, :], u_hF[:, hslot, :],
                                a_mb[sub][:, tsl], ALU.mult)),
                                reads=(hk, f"mb{sub}"), writes=(gmk,))
                            p.op("vector", (lambda en, sub=sub, fc=fc, hslot=hslot:
                                            en.tensor_tensor(
                                a_gm[:, sub * 8 + fc, :], u_hF[:, hslot, :],
                                a_gm[:, sub * 8 + fc, :], ALU.mult)),
                                reads=(hk, gmk), writes=(gmk,))
                        else:
                            p.op("scalar", (lambda en, ps=ps, hslot=hslot: en.activation(
                                u_hF[:, hslot, :], ps[:], AF.Relu,
                                scale=s_cst[:, C_SQH:C_SQH + 1])),
                                reads=(k, "cst"), writes=(hk,))
                            p.op("vector", (lambda en, sub=sub, fc=fc, hslot=hslot:
                                            en.tensor_tensor(
                                a_gm[:, sub * 8 + fc, :], u_hF[:, hslot, :],
                                u_hF[:, hslot, :], ALU.mult)),
                                reads=(hk,), writes=(gmk,))
                for c in range(NCH):
                    ps, k = take_ps()
                    first = True
                    for sub, e in ((0, e0), (1, e1)):
                        b = e % 2
                        for fc in range(8):
                            gmk = f"gm{sub * 8 + fc}"
                            p.op("tensor", (lambda en, ps=ps, b=b, fc=fc, c=c, sub=sub,
                                            first=first: en.matmul(
                                ps[:], u_w2e[b][:, fc, c * 128:(c + 1) * 128],
                                a_gm[:, sub * 8 + fc, :],
                                start=first, stop=(sub == 1 and fc == 7))),
                                reads=(f"w2e{b}", gmk), writes=(k,))
                            first = False
                    p.op("vector", (lambda en, ps=ps, c=c, tsl=tsl: en.tensor_tensor(
                        X[c][:, tsl], X[c][:, tsl], ps[:], ALU.add)),
                        reads=(f"X{c}", k), writes=(f"X{c}",))
            if pair < 4 and (pair + 1) in pairs:
                ffn_load_e(e0 + 2)
                ffn_load_e(e1 + 2)
            elif next_phase_loads is not None:
                next_phase_loads()

    def emit_attn(astop=None):
        p.op("vector", lambda e: e.memset(u_vt[:, :, 64:65], 1.0),
             writes=("vt_ones",))
        p.dma("gdma", lambda e: e.dma_start(
            out=a_wout[:], in_=d_woutT.rearrange("(k p) m -> p k m", p=128)),
            writes=("wout",))
        p.dma("gdma", lambda e: e.dma_start(
            out=u_qkvw[0][:], in_=d_wqkvh[0].rearrange("(k p) m -> p k m", p=128)),
            writes=("qkvw0",))
        p.dma("gdma", lambda e: e.dma_start(out=a_ph[0][:], in_=d_proj[0]),
              writes=("ph0",))
        for h in range(H):
            wb = h % 2
            for part, (dst, dk) in enumerate(((u_qh, "qh"), (u_kh, "kh"),
                                              (u_vh, "vh"))):
                for t in range(NT):
                    tsl = slice(t * TAU, (t + 1) * TAU)
                    ps, k = take_ps()
                    for kk in range(NCH):
                        p.op("tensor", (lambda e, ps=ps, wb=wb, kk=kk, part=part,
                                        tsl=tsl: e.matmul(
                            ps[0:64, :], u_qkvw[wb][:, kk, part * 64:(part + 1) * 64],
                            Y[kk][:, tsl], start=(kk == 0), stop=(kk == NCH - 1))),
                            reads=(f"qkvw{wb}", f"Y{kk}"), writes=(k,))
                    p.op("scalar", (lambda e, ps=ps, dst=dst, tsl=tsl: e.copy(
                        dst[:, tsl], ps[0:64, :])), reads=(k,), writes=(dk,))
            pp = 0
            for (srcap, srck, dstap, dstk) in ((u_qh, "qh", u_qf, "qf"),
                                               (u_kh, "kh", u_kfm, "kfm")):
                for mp in range(2):
                    for t in range(NT):
                        tsl = slice(t * TAU, (t + 1) * TAU)
                        ps, k = take_ps()
                        p.op("tensor", (lambda e, ps=ps, wb=wb, mp=mp, srcap=srcap,
                                        srck=srck, tsl=tsl: e.matmul(
                            ps[:], a_ph[wb][:, mp * 128:(mp + 1) * 128],
                            srcap[:, tsl], start=True, stop=True)),
                            reads=(f"ph{wb}", srck), writes=(k,))
                        if pp % 2 == 0:
                            e0, e1 = a_elu[0], a_elu[1]
                            e0k, e1k = "elu0", "elu1"
                            p.op("scalar", (lambda e, ps=ps, e0=e0: e.activation(
                                e0[:], ps[:], AF.Relu,
                                scale=s_cst[:, C_NEG1:C_NEG1 + 1])),
                                reads=(k, "cst"), writes=(e0k,))
                            p.op("scalar", (lambda e, e0=e0, e1=e1: e.activation(
                                e1[:], e0[:], AF.Exp,
                                scale=s_cst[:, C_NEG1:C_NEG1 + 1])),
                                reads=(e0k, "cst"), writes=(e1k,))
                        else:
                            e0, e1 = a_elu2[0], a_elu2[1]
                            e0k, e1k = "elu20", "elu21"
                            p.op("vector", (lambda e, ps=ps, e0=e0:
                                            e.tensor_single_scalar(
                                e0[:], ps[:], 0.0, ALU.min)),
                                reads=(k,), writes=(e0k,))
                            p.op("scalar", (lambda e, e0=e0, e1=e1: e.activation(
                                e1[:], e0[:], AF.Exp)),
                                reads=(e0k,), writes=(e1k,))
                        p.op("vector", (lambda e, ps=ps, dstap=dstap, mp=mp,
                                        tsl=tsl, e1=e1, e1k=e1k:
                                        e.scalar_tensor_tensor(
                            dstap[:, mp, tsl], ps[:], 0.0, e1[:],
                            ALU.max, ALU.add)),
                            reads=(k, e1k), writes=(dstk,))
                        pp += 1
            if h + 1 < H:
                nb = (h + 1) % 2
                p.dma("gdma", (lambda e, h=h, nb=nb: e.dma_start(
                    out=u_qkvw[nb][:],
                    in_=d_wqkvh[h + 1].rearrange("(k p) m -> p k m", p=128))),
                    writes=(f"qkvw{nb}",))
                p.dma("gdma", (lambda e, h=h, nb=nb: e.dma_start(
                    out=a_ph[nb][:], in_=d_proj[h + 1])), writes=(f"ph{nb}",))
            for half in range(2):
                for tt8 in range(8):
                    tt = half * 8 + tt8
                    p.op("tensor", (lambda e, tt=tt, tt8=tt8: e.transpose(
                        PT[:, 64 * tt8:64 * (tt8 + 1)],
                        u_vh[:, 128 * tt:128 * (tt + 1)], s_idbf[0:64, 0:64])),
                        reads=("vh", "idbf"), writes=("PT",))
                p.op("vector", (lambda e, half=half: e.tensor_copy(
                    u_vt[:, 8 * half:8 * (half + 1), 0:64],
                    PT[:].rearrange("p (tt m) -> p tt m", m=64)[:, 0:8, :])),
                    reads=("PT",), writes=("vt",))
            for mp in range(2):
                for half in range(2):
                    for tt8 in range(8):
                        tt = half * 8 + tt8
                        p.op("tensor", (lambda e, mp=mp, tt=tt, tt8=tt8: e.transpose(
                            PT[:, 128 * tt8:128 * (tt8 + 1)],
                            u_kfm[:, mp, 128 * tt:128 * (tt + 1)], s_idbf[:])),
                            reads=("kfm", "idbf"), writes=("PT",))
                    p.op("vector", (lambda e, half=half: e.tensor_copy(
                        u_kft[:, 8 * half:8 * (half + 1), :],
                        PT[:].rearrange("p (tt m) -> p tt m", m=128))),
                        reads=("PT",), writes=("kft",))
                ps, k = take_ps()
                for tt in range(16):
                    p.op("tensor", (lambda e, ps=ps, tt=tt: e.matmul(
                        ps[:, 0:65], u_kft[:, tt, :], u_vt[:, tt, :],
                        start=(tt == 0), stop=(tt == 15))),
                        reads=("kft", "vt", "vt_ones"), writes=(k,))
                p.op("vector", (lambda e, ps=ps, mp=mp: e.tensor_copy(
                    a_kvks[:, mp, :], ps[:, 0:65])), reads=(k,), writes=("kvks",))
            for t in range(NT):
                tsl = slice(t * TAU, (t + 1) * TAU)
                psn, kn = take_ps()
                for mp in range(2):
                    p.op("tensor", (lambda e, psn=psn, mp=mp, tsl=tsl: e.matmul(
                        psn[0:65, :], a_kvks[:, mp, :], u_qf[:, mp, tsl],
                        start=(mp == 0), stop=(mp == 1))),
                        reads=("kvks", "qf"), writes=(kn,))
                p.op("vector", (lambda e, psn=psn: e.tensor_scalar_add(
                    a_rS[0:1, :], psn[64:65, :], EPS_ATTN)),
                    reads=(kn,), writes=("rS",))
                p.op("scalar", (lambda e: e.activation(
                    a_rS[0:1, :], a_rS[0:1, :], AF.Ln)),
                    reads=("rS",), writes=("rS",))
                p.op("scalar", (lambda e, tsl=tsl: e.activation(
                    a_z[0:1, tsl], a_rS[0:1, :], AF.Exp,
                    scale=s_cst[0:1, C_NEG1:C_NEG1 + 1])),
                    reads=("rS", "cst"), writes=("z",))
                psz, kz = take_ps()
                p.op("tensor", (lambda e, psz=psz, tsl=tsl: e.matmul(
                    psz[0:64, :], s_ones[0:1, 0:64], a_z[0:1, tsl],
                    start=True, stop=True)), reads=("onesr", "z"), writes=(kz,))
                p.op("scalar", (lambda e, psz=psz: e.copy(
                    a_zb[:], psz[0:64, :])), reads=(kz,), writes=("zb",))
                if h % 2 == 0:
                    p.op("vector", (lambda e, psn=psn, h=h, tsl=tsl: e.tensor_tensor(
                        u_att[0:64, h // 2, tsl], psn[0:64, :], a_zb[:], ALU.mult)),
                        reads=(kn, "zb"), writes=("att",))
                else:
                    p.op("vector", (lambda e, psn=psn, tsl=tsl: e.tensor_tensor(
                        u_attst[:, tsl], psn[0:64, :], a_zb[:], ALU.mult)),
                        reads=(kn, "zb"), writes=("attst",))
            if h % 2 == 1:
                p.dma("gdma", (lambda e, h=h: e.dma_start(
                    out=u_att[64:128, h // 2, :], in_=u_attst[:])),
                    reads=("attst",), writes=("att",))
        if astop == "heads":
            for c in range(NCH):
                p.dma("gdma", (lambda e, c=c: e.dma_start(
                    out=d_out[c * 128:(c + 1) * 128, :], in_=u_att[:, c, :])),
                    reads=("att",), writes=(f"o{c}",))
            return
        for t in range(NT):
            tsl = slice(t * TAU, (t + 1) * TAU)
            for c in range(NCH):
                ps, k = take_ps()
                for kk in range(NCH):
                    p.op("tensor", (lambda e, ps=ps, kk=kk, c=c, tsl=tsl: e.matmul(
                        ps[:], a_wout[:, kk, c * 128:(c + 1) * 128],
                        u_att[:, kk, tsl], start=(kk == 0), stop=(kk == NCH - 1))),
                        reads=("wout", "att"), writes=(k,))
                p.op("vector", (lambda e, ps=ps, c=c, tsl=tsl: e.scalar_tensor_tensor(
                    Y[c][:, tsl], ps[:], s_bout[:, c:c + 1], Y[c][:, tsl],
                    ALU.add, ALU.add)),
                    reads=(k, "bout", f"Y{c}"), writes=(f"Y{c}",))
        p.dma("gdma", lambda e: e.dma_start(out=a_gz[:], in_=d_gz), writes=("gz",))
        p.dma("gdma", lambda e: e.dma_start(out=a_beta[:], in_=d_beta), writes=("beta",))
        if astop == "xattn":
            for c in range(NCH):
                p.dma("gdma", (lambda e, c=c: e.dma_start(
                    out=d_out[c * 128:(c + 1) * 128, :], in_=Y[c][:])),
                    reads=(f"Y{c}",), writes=(f"o{c}",))
            return
        emit_ln([Y[c][:] for c in range(NCH)], [f"Y{c}" for c in range(NCH)],
                True, True, [u_y3[:, c, :] for c in range(NCH)],
                [f"y3{c}" for c in range(NCH)])
        if astop == "y3":
            for c in range(NCH):
                p.dma("gdma", (lambda e, c=c: e.dma_start(
                    out=d_out[c * 128:(c + 1) * 128, :], in_=u_y3[:, c, :])),
                    reads=(f"y3{c}",), writes=(f"o{c}",))
            return
        if astop == "mlp0":
            p.dma("gdma", (lambda e: e.dma_start(
                out=d_out[640:768, 0:2048], in_=u_y3[:, 0, :])),
                reads=("y30",), writes=("oo5b",))
        p.dma("gdma", lambda e: e.dma_start(
            out=u_mw1[0][:],
            in_=d_w1T.rearrange("(k p) m -> p k m", p=128)[:, :, 0:512]),
            writes=("mw10",))
        gi = 0
        for q in range(4):
            qsl = slice(q * TAU, (q + 1) * TAU)
            for g in range(8):
                wb1 = gi % 2
                if gi + 1 < 32:
                    g2 = (gi + 1) % 8
                    p.dma("gdma", (lambda e, g2=g2, nb=(gi + 1) % 2: e.dma_start(
                        out=u_mw1[nb][:],
                        in_=d_w1T.rearrange("(k p) m -> p k m", p=128)
                        [:, :, g2 * 512:(g2 + 1) * 512])),
                        writes=(f"mw1{(gi + 1) % 2}",))
                p.dma("gdma", (lambda e, g=g: e.dma_start(
                    out=u_mw2[:],
                    in_=d_w2T[g * 512:(g + 1) * 512, :]
                    .rearrange("(k p) m -> p k m", p=128))),
                    writes=("mw2",))
                gi += 1
                for fcg in range(4):
                    fc = 4 * g + fcg
                    ps, k = take_ps()
                    for kk in range(NCH):
                        p.op("tensor", (lambda e, ps=ps, wb1=wb1, kk=kk, fcg=fcg,
                                        qsl=qsl: e.matmul(
                            ps[:], u_mw1[wb1][:, kk, fcg * 128:(fcg + 1) * 128],
                            u_y3[:, kk, qsl], start=(kk == 0), stop=(kk == NCH - 1))),
                            reads=(f"mw1{wb1}", f"y3{kk}"), writes=(k,))
                    p.op("scalar", (lambda e, ps=ps, fc=fc, fcg=fcg: e.activation(
                        a_hM[:, fcg, :], ps[:], AF.Gelu, bias=s_b1[:, fc:fc + 1])),
                        reads=(k, "b1"), writes=(f"hM{fcg}",))
                    p.op("vector", (lambda e, fcg=fcg: e.tensor_copy(
                        a_hM2[:, fcg, :], a_hM[:, fcg, :])),
                        reads=(f"hM{fcg}",), writes=(f"hN{fcg}",))
                for c in range(NCH):
                    ps, k = take_ps()
                    for fcg in range(4):
                        p.op("tensor", (lambda e, ps=ps, fcg=fcg, c=c: e.matmul(
                            ps[:], u_mw2[:, fcg, c * 128:(c + 1) * 128],
                            a_hM2[:, fcg, :], start=(fcg == 0), stop=(fcg == 3))),
                            reads=("mw2", f"hN{fcg}"), writes=(k,))
                    if g == 0:
                        p.op("vector", (lambda e, ps=ps, c=c, qsl=qsl: e.tensor_tensor(
                            Y[c][:, qsl], u_y3[:, c, qsl], ps[:], ALU.add)),
                            reads=(f"y3{c}", k), writes=(f"Y{c}",))
                    elif g < 7:
                        p.op("vector", (lambda e, ps=ps, c=c, qsl=qsl: e.tensor_tensor(
                            Y[c][:, qsl], Y[c][:, qsl], ps[:], ALU.add)),
                            reads=(f"Y{c}", k), writes=(f"Y{c}",))
                    else:
                        p.op("vector", (lambda e, ps=ps, c=c, qsl=qsl:
                                        e.scalar_tensor_tensor(
                            Y[c][:, qsl], ps[:], s_b2[:, c:c + 1], Y[c][:, qsl],
                            ALU.add, ALU.add)),
                            reads=(k, "b2", f"Y{c}"), writes=(f"Y{c}",))
                if astop == "mlp0" and q == 0 and g == 1:
                    p.dma("gdma", (lambda e: e.dma_start(
                        out=d_out[0:128, 0:2048],
                        in_=a_hM[:].rearrange("p a b -> p (a b)"))),
                        reads=("hM0", "hM1", "hM2", "hM3"), writes=("oo0",))
                    p.dma("gdma", (lambda e: e.dma_start(
                        out=d_out[128:256, 0:2048],
                        in_=u_acc[:, 0:4, :].rearrange("p a b -> p (a b)"))),
                        reads=("acc",), writes=("oo1",))
                    p.dma("gdma", (lambda e: e.dma_start(
                        out=d_out[256:384, 0:512], in_=u_mw1[1][:, 0, :])),
                        reads=("mw11",), writes=("oo2",))
                    p.dma("gdma", (lambda e: e.dma_start(
                        out=d_out[384:512, 0:1024], in_=u_mw2[:, 0, :])),
                        reads=("mw2",), writes=("oo3",))
                    p.dma("gdma", (lambda e: e.dma_start(
                        out=d_out[512:640, 0:2048], in_=u_y3[:, 0, :])),
                        reads=("y30",), writes=("oo4",))
                    return
            if astop == "mlpq":
                p.dma("sdma", (lambda e, q=q: e.dma_start(
                    out=d_out[q * 128:(q + 1) * 128, 0:1024],
                    in_=Y[0][:].bitcast(F32))),
                    reads=("Y0",), writes=(f"oq{q}",))
                p.dma("sdma", (lambda e, q=q: e.dma_start(
                    out=d_out[512 + q * 128:512 + (q + 1) * 128, :],
                    in_=u_acc[:, 0:4, :].rearrange("p a b -> p (a b)"))),
                    reads=("acc",), writes=(f"oq2{q}",))
        if astop == "mlpq":
            return
        if astop == "x2":
            for c in range(NCH):
                p.dma("gdma", (lambda e, c=c: e.dma_start(
                    out=d_out[c * 128:(c + 1) * 128, :], in_=Y[c][:])),
                    reads=(f"Y{c}",), writes=(f"o{c}",))
            return
        emit_ln([Y[c][:] for c in range(NCH)], [f"Y{c}" for c in range(NCH)],
                True, True, [u_y3[:, 0, :]], ["lnscr"], add_into_x=True,
                dbg_rows=(astop == "ln4s"))

    def _mac_ln():
        emit_ln([X[c][:] for c in range(NCH)], [f"X{c}" for c in range(NCH)],
                False, False, [Y[c][:] for c in range(NCH)],
                [f"Y{c}" for c in range(NCH)])

    stages = [
        ("ln_a", _mac_ln),
        ("ffn1", emit_ffn),
        ("ln_b", _mac_ln),
        ("attn", emit_attn),
        ("pre2", lambda: (ffn_load_e(0), ffn_load_e(1))),
        ("ln_c", _mac_ln),
        ("ffn2", emit_ffn),
    ]
    if stop_after == "wro":
        p.op("vector", lambda e: e.tensor_copy(
            a_ltok[:, 0:64], s_wro[:].rearrange("p k m -> p (k m)")),
            reads=("wro", "w1e0", "w1e1", "w2e0", "w2e1", "cst", "b1", "b2",
                   "bout", "oneh", "selr", "idbf", "id32"),
            writes=("ltok",))
        p.dma("gdma", lambda e: e.dma_start(
            out=d_out[0:128, 0:64], in_=a_ltok[:, 0:64]),
            reads=("ltok",), writes=("o0",))
        stages = []
    elif stop_after in ("lg1", "route1", "routedbg"):
        _mac_ln()
        emit_routing()
        if stop_after == "lg1":
            p.dma("gdma", lambda e: e.dma_start(out=d_out[0:8, :], in_=a_lg[:]),
                  reads=("lg",), writes=("o0",))
        elif stop_after == "route1":
            p.dma("gdma", lambda e: e.dma_start(out=d_out[0:8, :], in_=a_mrow[:]),
                  reads=("mrow",), writes=("o0",))
        else:
            for nm, ap_, key, w_ in (("wrk", a_wrk[:], "wrk", 128),
                                     ("dn", a_dn[:], "dn", 16),
                                     ("rc", a_rc[:], "rc", 16),
                                     ("m8", a_m8[:], "m8", 128),
                                     ("sel", a_sel[:], "sel", 128),
                                     ("ltok", a_ltok[:], "ltok", 128),
                                     ("nm", a_nm[:], "nm", 16)):
                pass
            col = [0]

            def dump(ap_, key, w_):
                c0 = col[0]
                p.dma("gdma", (lambda e, ap_=ap_, c0=c0, w_=w_: e.dma_start(
                    out=d_out[0:128, c0:c0 + w_], in_=ap_)),
                    reads=(key,), writes=("o0",))
                col[0] += w_
            dump(a_wrk[:], "wrk", 128)
            dump(a_dn[:], "dn", 16)
            dump(a_rc[:], "rc", 16)
            dump(a_m8[:], "m8", 128)
            dump(a_sel[:], "sel", 128)
            dump(a_ltok[:], "ltok", 128)
            dump(a_nm[:], "nm", 16)
        stages = []
    if stop_after in ("heads", "xattn", "y3", "x2", "mlp0", "mlpq"):
        _mac_ln()
        emit_ffn()
        _mac_ln()
        emit_attn(astop=stop_after)
        stages = []
    if stop_after == "ln4s":
        _mac_ln()
        emit_ffn()
        _mac_ln()
        emit_attn(astop="ln4s")
        p.dma("gdma", (lambda e: e.dma_start(out=d_out[0:2, :], in_=a_rbf[:])),
              reads=("rbf",), writes=("oo0",))
        stages = []
    if stop_after == "ln4r":
        _mac_ln()
        emit_ffn()
        _mac_ln()
        emit_attn()
        p.dma("gdma", (lambda e: e.dma_start(out=d_out[0:2, :], in_=a_rbf[:])),
              reads=("rbf",), writes=("oo0",))
        p.dma("gdma", (lambda e: e.dma_start(out=d_out[2:4, 0:1024],
                                             in_=a_gz[:, 0:1024])),
              reads=("gz",), writes=("oo1",))
        p.dma("sdma", (lambda e: e.dma_start(out=d_out[128:256, :], in_=X[0][:])),
              reads=("X0",), writes=("oo2",))
        stages = []
    if stop_after == "ffnp0d":
        _mac_ln()
        emit_ffn(pairs=(0,))
        def dmp(row, ap_, key):
            p.dma("gdma", (lambda e, ap_=ap_, row=row: e.dma_start(
                out=d_out[row:row + 128, 0:ap_.shape[-1] if False else 2048],
                in_=ap_)), reads=(key,), writes=(f"o{row}",))
        p.dma("gdma", (lambda e: e.dma_start(out=d_out[0:128, :], in_=a_mb[0][:])),
              reads=("mb0",), writes=("oo0",))
        p.dma("gdma", (lambda e: e.dma_start(
            out=d_out[256:384, :], in_=a_gm[:, 0:4, :].rearrange("p a b -> p (a b)"))),
            reads=("gm0", "gm1", "gm2", "gm3"), writes=("oo2",))
        p.dma("gdma", (lambda e: e.dma_start(
            out=d_out[384:512, :], in_=a_gm[:, 4:8, :].rearrange("p a b -> p (a b)"))),
            reads=("gm4", "gm5", "gm6", "gm7"), writes=("oo3",))
        p.dma("sdma", (lambda e: e.dma_start(out=d_out[512:640, :], in_=X[0][:])),
              reads=("X0",), writes=("oo4",))
        p.dma("gdma", (lambda e: e.dma_start(out=d_out[640:648, :], in_=a_mrow[:])),
              reads=("mrow",), writes=("oo5",))
        p.dma("gdma", (lambda e: e.dma_start(
            out=d_out[648:776, :], in_=a_gm[:, 8:12, :].rearrange("p a b -> p (a b)"))),
            reads=("gm8", "gm9", "gm10", "gm11"), writes=("oo6",))
        p.dma("gdma", (lambda e: e.dma_start(
            out=d_out[776:904, :], in_=a_gm[:, 12:16, :].rearrange("p a b -> p (a b)"))),
            reads=("gm12", "gm13", "gm14", "gm15"), writes=("oo7b",))
        p.dma("gdma", (lambda e: e.dma_start(
            out=d_out[128:256, 0:1024], in_=u_w2e[0][:, 0, :])),
            reads=("w2e0",), writes=("oo1",))
        for rep in range(2):
            ps_d, k_d = take_ps()
            first = True
            for sub in (0, 1):
                for fc in range(8):
                    p.op("tensor", (lambda en, ps_d=ps_d, sub=sub, fc=fc,
                                    first=first: en.matmul(
                        ps_d[:], u_w2e[sub][:, fc, 0:128],
                        a_gm[:, sub * 8 + fc, :],
                        start=first, stop=(sub == 1 and fc == 7))),
                        reads=(f"w2e{sub}", f"gm{sub * 8 + fc}"), writes=(k_d,))
                    first = False
            p.op("vector", (lambda e, ps_d=ps_d, rep=rep: e.tensor_copy(
                a_wrk[:, 0:128], ps_d[:, 0:128])), reads=(k_d,), writes=("wrk",))
            p.dma("gdma", (lambda e, rep=rep: e.dma_start(
                out=d_out[896:1024, 128 * rep:128 * (rep + 1)],
                in_=a_wrk[:, 0:128])),
                reads=("wrk",), writes=(f"oo8{rep}",))
        for slot in range(16):
            p.op("vector", (lambda e, slot=slot: e.tensor_copy(
                Y[slot // 4][:, (slot % 4) * 512:(slot % 4 + 1) * 512],
                a_gm[:, slot, :])),
                reads=(f"gm{slot}",), writes=(f"Y{slot // 4}",))
        ps_e, k_e = take_ps()
        first = True
        for sub in (0, 1):
            for fc in range(8):
                slot = sub * 8 + fc
                p.op("tensor", (lambda en, ps_e=ps_e, sub=sub, fc=fc, slot=slot,
                                first=first: en.matmul(
                    ps_e[:], u_w2e[sub][:, fc, 0:128],
                    Y[slot // 4][:, (slot % 4) * 512:(slot % 4 + 1) * 512],
                    start=first, stop=(sub == 1 and fc == 7))),
                    reads=(f"w2e{sub}", f"Y{slot // 4}"), writes=(k_e,))
                first = False
        p.op("vector", (lambda e, ps_e=ps_e: e.tensor_copy(
            a_sel[:, 0:128], ps_e[:, 0:128])), reads=(k_e,), writes=("sel",))
        p.dma("gdma", (lambda e: e.dma_start(
            out=d_out[896:1024, 256:384], in_=a_sel[:, 0:128])),
            reads=("sel",), writes=("oo9",))
        ps_f, k_f = take_ps()
        p.op("tensor", (lambda en, ps_f=ps_f: en.matmul(
            ps_f[:], u_w2e[1][:, 1, 0:128], a_gm[:, 9, :],
            start=True, stop=True)), reads=("w2e1", "gm9"), writes=(k_f,))
        p.op("vector", (lambda e, ps_f=ps_f: e.tensor_copy(
            a_ltok[:, 0:128], ps_f[:, 0:128])), reads=(k_f,), writes=("ltok",))
        p.dma("gdma", (lambda e: e.dma_start(
            out=d_out[896:1024, 384:512], in_=a_ltok[:, 0:128])),
            reads=("ltok",), writes=("ooA",))
        p.dma("gdma", (lambda e: e.dma_start(
            out=d_out[0:128, 1024:2048], in_=u_w2e[1][:, 1, :])),
            reads=("w2e1",), writes=("ooB",))
        stages = []
    if stop_after == "ffninc":
        _mac_ln()
        emit_ffn(pairs=(0,))
        for gl in range(16):
            ps_g, k_g = take_ps()
            first = True
            for s in range(gl + 1):
                sub, fc = s // 8, s % 8
                p.op("tensor", (lambda en, ps_g=ps_g, sub=sub, fc=fc, s=s,
                                first=first, gl=gl: en.matmul(
                    ps_g[:], u_w2e[sub][:, fc, 0:128],
                    a_gm[:, s, :],
                    start=first, stop=(s == gl))),
                    reads=(f"w2e{sub}", f"gm{s}"), writes=(k_g,))
                first = False
            p.op("vector", (lambda e, ps_g=ps_g, gl=gl: e.tensor_copy(
                a_wrk[:, 0:128], ps_g[:, 0:128])), reads=(k_g,), writes=("wrk",))
            p.dma("gdma", (lambda e, gl=gl: e.dma_start(
                out=d_out[(gl // 2) * 128:(gl // 2 + 1) * 128,
                          1536 + (gl % 2) * 128:1536 + (gl % 2 + 1) * 128],
                in_=a_wrk[:, 0:128])),
                reads=("wrk",), writes=(f"oi{gl}",))
        for tag, slots in (("s14", (14,)), ("p1314", (13, 14)), ("s15", (15,))):
            ps_h, k_h = take_ps()
            first = True
            for s in slots:
                sub, fc = s // 8, s % 8
                p.op("tensor", (lambda en, ps_h=ps_h, sub=sub, fc=fc, s=s,
                                first=first, last=(s == slots[-1]): en.matmul(
                    ps_h[:], u_w2e[sub][:, fc, 0:128],
                    a_gm[:, s, :], start=first, stop=last)),
                    reads=(f"w2e{sub}", f"gm{s}"), writes=(k_h,))
                first = False
            p.op("vector", (lambda e, ps_h=ps_h: e.tensor_copy(
                a_wrk[:, 0:128], ps_h[:, 0:128])), reads=(k_h,), writes=("wrk",))
            row = {"s14": 1792, "p1314": 1792 + 0, "s15": 1792}[tag]
            col = {"s14": 1024, "p1314": 1152, "s15": 1280}[tag]
            p.dma("gdma", (lambda e, col=col: e.dma_start(
                out=d_out[896:1024, col:col + 128], in_=a_wrk[:, 0:128])),
                reads=("wrk",), writes=(f"oh{tag}",))
        for s in range(16):
            p.dma("gdma", (lambda e, s=s: e.dma_start(
                out=d_out[(s // 2) * 128:(s // 2 + 1) * 128,
                          (s % 2) * 512:(s % 2 + 1) * 512],
                in_=a_gm[:, s, :])),
                reads=(f"gm{s}",), writes=(f"og{s}",))
        stages = []
    if stop_after in ("ffn1p0", "ffn1sh"):
        _mac_ln()
        if stop_after == "ffn1p0":
            emit_ffn(pairs=(0,))
        else:
            ffn_load_e(8)
            ffn_load_e(9)
            emit_ffn(pairs=(4,))
        for c in range(NCH):
            p.dma("sdma", (lambda e, c=c: e.dma_start(
                out=d_out[c * 128:(c + 1) * 128, :], in_=X[c][:])),
                reads=(f"X{c}",), writes=(f"o{c}",))
        stages = []
    for sname, sfn in stages:
        sfn()
        if stop_after == sname:
            break
    if stop_after in ("wro", "lg1", "route1", "routedbg", "ffn1p0", "ffn1sh",
                      "ffnp0d", "ffninc", "heads", "xattn", "y3", "x2", "mlp0",
                      "mlpq", "ln4r", "ln4s"):
        pass
    elif stop_after in ("ln_a", "ln_b", "ln_c"):
        for c in range(NCH):
            p.dma("gdma", (lambda e, c=c: e.dma_start(
                out=d_out[c * 128:(c + 1) * 128, :], in_=Y[c][:])),
                reads=(f"Y{c}",), writes=(f"o{c}",))
    else:
        for c in range(NCH):
            p.dma("sdma", (lambda e, c=c: e.dma_start(
                out=d_out[c * 128:(c + 1) * 128, :], in_=X[c][:])),
                reads=(f"X{c}",), writes=(f"o{c}",))

    with (
        nc.semaphore("smT") as smT, nc.semaphore("smV") as smV,
        nc.semaphore("smS") as smS, nc.semaphore("smG") as smG,
        nc.semaphore("smY") as smY, nc.semaphore("dmG") as dmG,
        nc.semaphore("dmY") as dmY,
    ):
        sems = {"tensor": smT, "vector": smV, "scalar": smS,
                "gpsimd": smG, "sync": smY}
        dma_sems = {"gdma": dmG, "sdma": dmY}
        allsems = [smT, smV, smS, smG, smY, dmG, dmY]
        nc.all_engine_barrier()
        rng = range(min(s.num for s in allsems), max(s.num for s in allsems) + 1)
        nc.gpsimd.dma_reset(rng)
        nc.gpsimd.sem_clear(rng)
        nc.all_engine_barrier()
        replay = p.finalize(sems, dma_sems)
        n_sdma = p.nidx["sdma"]
        n_gdma = p.nidx["gdma"]
        with nc.Block() as block:
            block.tensor(replay("tensor"))
            block.vector(replay("vector"))
            block.scalar(replay("scalar"))

            def gpsbody(engobj):
                replay("gpsimd")(engobj)
                engobj.wait_ge(dmG, 16 * n_gdma)
            block.gpsimd(gpsbody)

            def syncbody(engobj):
                replay("sync")(engobj)
                engobj.wait_ge(dmY, 16 * n_sdma)
            block.sync(syncbody)
    return nc


def _host_prep(inputs):
    f32 = np.float32
    x = np.asarray(inputs["L0_attn_x"], f32)
    wqkv = np.asarray(inputs["L0_attn_w_qkv"], f32)
    proj = np.asarray(inputs["L0_attn_proj"], f32)
    wout = np.asarray(inputs["L0_attn_w_out"], f32)
    bout = np.asarray(inputs["L0_attn_b_out"], f32)
    gamma = np.asarray(inputs["L0_attn_gamma"], f32)
    beta = np.asarray(inputs["L0_attn_beta"], f32)
    w1 = np.asarray(inputs["L0_attn_w1"], f32)
    b1 = np.asarray(inputs["L0_attn_b1"], f32)
    w2 = np.asarray(inputs["L0_attn_w2"], f32)
    b2 = np.asarray(inputs["L0_attn_b2"], f32)
    wr = np.asarray(inputs["L0_ffn_w_router"], f32)
    w1s = np.asarray(inputs["L0_ffn_w1_stack"], f32)
    w2s = np.asarray(inputs["L0_ffn_w2_stack"], f32)
    ws1 = np.asarray(inputs["L0_ffn_ws1"], f32)
    ws2 = np.asarray(inputs["L0_ffn_ws2"], f32)

    qT = wqkv[0:D].T
    kT = wqkv[D:2 * D].T
    vT = wqkv[2 * D:3 * D].T
    wqkvh = np.empty((H, D, 192), f32)
    for h in range(H):
        wqkvh[h, :, 0:64] = qT[:, h * 64:(h + 1) * 64]
        wqkvh[h, :, 64:128] = kT[:, h * 64:(h + 1) * 64]
        wqkvh[h, :, 128:192] = vT[:, h * 64:(h + 1) * 64]

    w1sT = np.ascontiguousarray(w1s.transpose(0, 2, 1))
    w2sT = np.ascontiguousarray(w2s.transpose(0, 2, 1))
    ws1T = np.ascontiguousarray(ws1.T)
    ws2T = np.ascontiguousarray(ws2.T)
    w1all = np.concatenate(
        [w1sT, ws1T[:, 0:D][None], ws1T[:, D:2 * D][None]], axis=0)
    w2all = np.concatenate(
        [w2sT, ws2T[0:D, :][None], ws2T[D:2 * D, :][None]], axis=0)

    oneh = np.zeros((8, D), f32)
    for e in range(8):
        oneh[e, 128 * e:128 * (e + 1)] = 1.0
    csts = np.zeros((128, 8), f32)
    csts[:, 0] = EPS_LN
    csts[:, 1] = -0.5
    csts[:, 2] = np.sqrt(0.5)
    csts[:, 3] = -1.0
    gz = np.zeros((2, 2 * D), f32)
    gz[0, 0:D] = gamma
    gz[1, D:2 * D] = gamma
    selrow = np.zeros((2, 256), f32)
    selrow[0, 0:128] = 1.0
    selrow[1, 128:256] = 1.0

    common = {
        "wqkvh": wqkvh, "proj": np.ascontiguousarray(proj),
        "woutT": np.ascontiguousarray(wout.T),
        "w1T": np.ascontiguousarray(w1.T),
        "w2T": np.ascontiguousarray(w2.T),
        "wrT": np.ascontiguousarray(wr.T),
        "w1all": w1all, "w2all": w2all,
        "boutc": np.ascontiguousarray(bout.reshape(8, 128).T),
        "b2c": np.ascontiguousarray(b2.reshape(8, 128).T),
        "b1c": np.ascontiguousarray(b1.reshape(32, 128).T),
        "gz": gz, "betar": beta.reshape(1, D), "selrow": selrow,
        "oneh": oneh, "idf": np.eye(128, dtype=f32), "csts": csts,
    }
    in_maps = []
    for b in range(NCORES):
        m = dict(common)
        m["x"] = np.ascontiguousarray(x[b].T)
        in_maps.append(m)
    return in_maps


def kernel(**inputs):
    in_maps = _host_prep(inputs)
    nc = build_program()
    res = run_bass_kernel_spmd(nc, in_maps, core_ids=list(range(NCORES)))
    out = np.stack([np.ascontiguousarray(r["o"].T) for r in res.results], axis=0)
    return out


# version 12
# speedup vs baseline: 1.0380x; 1.0380x over previous
import numpy as np
import ml_dtypes
import concourse.bass as bass
import concourse.mybir as mybir
from concourse.bass_utils import run_bass_kernel_spmd

F32 = mybir.dt.float32
BF16 = mybir.dt.bfloat16
AF = mybir.ActivationFunctionType
ALU = mybir.AluOpType
AXX = mybir.AxisListType.X

NCORES = 8
D = 1024
T = 2048
NCH = 8
NT = 4
TAU = 512
H = 16
HD = 64
FP = 256
NE = 10
EPS_LN = 1e-5
EPS_ATTN = 1e-8

ENGS = ("tensor", "vector", "scalar", "gpsimd", "sync")
DQUEUES = {"gdma": "gpsimd", "sdma": "sync"}


class Prog:

    def __init__(self):
        self.items = []
        self.nidx = {k: 0 for k in list(ENGS) + list(DQUEUES)}
        self.last_w = {}
        self.readers = {}
        self.region = {}
        self.arena_keys = {}
        self.forced = set()

    def reg(self, key, arena, lo, hi):
        self.region[key] = (arena, lo, hi)
        self.arena_keys.setdefault(arena, []).append(key)

    def _overlaps(self, key):
        r = self.region.get(key)
        if r is None:
            return ()
        a, lo, hi = r
        out = []
        for k2 in self.arena_keys[a]:
            if k2 == key:
                continue
            _, lo2, hi2 = self.region[k2]
            if lo < hi2 and lo2 < hi:
                out.append(k2)
        return out

    def _deps(self, reads, writes, scalar_reads):
        deps = {}

        def dep_read(r, force):
            t = self.last_w.get(r)
            if t is not None:
                if force or t in self.forced:
                    deps[t] = True
                else:
                    deps.setdefault(t, False)

        def dep_write(w):
            t = self.last_w.get(w)
            if t is not None:
                if t in self.forced:
                    deps[t] = True
                else:
                    deps.setdefault(t, False)
            for t2 in self.readers.get(w, ()):
                deps.setdefault(t2, False)

        for r in reads:
            dep_read(r, False)
            for k2 in self._overlaps(r):
                dep_read(k2, False)
        for r in scalar_reads:
            dep_read(r, True)
            for k2 in self._overlaps(r):
                dep_read(k2, True)
        for w in writes:
            dep_write(w)
            for k2 in self._overlaps(w):
                dep_write(k2)
        return deps

    def _post(self, tok, reads, writes, scalar_reads):
        for w in writes:
            self.last_w[w] = tok
            self.readers[w] = []
        for r in list(reads) + list(scalar_reads):
            self.readers.setdefault(r, []).append(tok)

    def op(self, eng, fn, reads=(), writes=(), scalar_reads=(), sync_out=False):
        assert eng in ENGS
        deps = self._deps(reads, writes, scalar_reads)
        for t in list(deps):
            if t[0] in ("vector", "scalar") or (t[0] == eng and eng != "tensor"):
                deps[t] = True
        idx = self.nidx[eng]
        self.nidx[eng] += 1
        tok = (eng, idx)
        self.items.append(dict(kind="op", eng=eng, fn=fn, idx=idx, deps=deps))
        self._post(tok, reads, writes, scalar_reads)
        if sync_out:
            self.forced.add(tok)
        return tok

    SERIALIZE_DMA = True

    def dma(self, queue, fn, reads=(), writes=()):
        assert queue in DQUEUES
        deps = self._deps(reads, writes, ())
        if self.SERIALIZE_DMA and self.nidx[queue] > 0:
            deps.setdefault((queue, self.nidx[queue] - 1), False)
        idx = self.nidx[queue]
        self.nidx[queue] += 1
        tok = (queue, idx)
        self.items.append(dict(kind="dma", eng=DQUEUES[queue], queue=queue,
                               fn=fn, idx=idx, deps=deps))
        self._post(tok, reads, writes, ())
        return tok

    def finalize(self, sems, dma_sems):
        marked = {e: set() for e in ENGS}
        for it in self.items:
            for (pe, pidx), force in it["deps"].items():
                if pe in ENGS:
                    if pe == it["eng"] and not force:
                        continue
                    marked[pe].add(pidx)
        rank = {}
        for e in ENGS:
            for i, idx in enumerate(sorted(marked[e])):
                rank[(e, idx)] = i + 1
        per_eng = {e: [] for e in ENGS}
        for it in self.items:
            per_eng[it["eng"]].append(it)

        def replay(ename):
            def body(engobj):
                hwm = {}
                for it in per_eng[ename]:
                    for (pe, pidx), force in sorted(it["deps"].items()):
                        if pe in ENGS:
                            if pe == ename and not force:
                                continue
                            sem, val = sems[pe], rank[(pe, pidx)]
                        else:
                            sem, val = dma_sems[pe], 16 * (pidx + 1)
                        if hwm.get(sem.name, 0) < val:
                            engobj.wait_ge(sem, val)
                            hwm[sem.name] = val
                    inst = it["fn"](engobj)
                    if it["kind"] == "dma":
                        inst.then_inc(dma_sems[it["queue"]], 16)
                    elif it["idx"] in marked[ename]:
                        inst.then_inc(sems[ename], 1)
            return body

        return replay


def build_program(stop_after=None):
    nc = bass.Bass("TRN2", target_bir_lowering=False)

    d_x = nc.dram_tensor("x", [D, T], F32, kind="ExternalInput").ap()
    d_wqkvh = nc.dram_tensor("wqkvh", [H, D, 192], F32, kind="ExternalInput").ap()
    d_proj = nc.dram_tensor("proj", [H, HD, FP], F32, kind="ExternalInput").ap()
    d_woutT = nc.dram_tensor("woutT", [D, D], F32, kind="ExternalInput").ap()
    d_w1T = nc.dram_tensor("w1T", [D, 4 * D], F32, kind="ExternalInput").ap()
    d_w2T = nc.dram_tensor("w2T", [4 * D, D], F32, kind="ExternalInput").ap()
    d_wrT = nc.dram_tensor("wrT", [D, 8], F32, kind="ExternalInput").ap()
    d_w1all = nc.dram_tensor("w1all", [NE, D, D], F32, kind="ExternalInput").ap()
    d_w2all = nc.dram_tensor("w2all", [NE, D, D], F32, kind="ExternalInput").ap()
    d_boutc = nc.dram_tensor("boutc", [128, 8], F32, kind="ExternalInput").ap()
    d_b2c = nc.dram_tensor("b2c", [128, 8], F32, kind="ExternalInput").ap()
    d_b1c = nc.dram_tensor("b1c", [128, 32], F32, kind="ExternalInput").ap()
    d_gz = nc.dram_tensor("gz", [2, 2 * D], F32, kind="ExternalInput").ap()
    d_beta = nc.dram_tensor("betar", [1, D], F32, kind="ExternalInput").ap()
    d_sel = nc.dram_tensor("selrow", [2, 256], F32, kind="ExternalInput").ap()
    d_oneh = nc.dram_tensor("oneh", [8, D], F32, kind="ExternalInput").ap()
    d_idf = nc.dram_tensor("idf", [128, 128], F32, kind="ExternalInput").ap()
    d_csts = nc.dram_tensor("csts", [128, 8], F32, kind="ExternalInput").ap()
    d_out = nc.dram_tensor("o", [D, T], F32, kind="ExternalOutput").ap()

    X = [nc.alloc_sbuf_tensor(f"Xc{c}", [128, T], F32) for c in range(NCH)]
    Y = [nc.alloc_sbuf_tensor(f"Yc{c}", [128, T], BF16) for c in range(NCH)]
    U = nc.alloc_sbuf_tensor("U", [128, 73728], mybir.dt.uint8)
    A2 = nc.alloc_sbuf_tensor("A2", [128, 32768], mybir.dt.uint8)
    ubase = nc.lookup_mloc(U).addr
    abase = nc.lookup_mloc(A2).addr

    p = Prog()

    def at(name, shape, dtype, base, off, key, arena, nbytes):
        t = nc.alloc_sbuf_tensor_at(name, shape, dtype, offset=base + off)
        if isinstance(key, str):
            p.reg(key, arena, off, off + nbytes)
        return t

    u_w1e = [at(f"u_w1e{i}", [128, 8, D], BF16, ubase, 16384 * i,
                f"w1e{i}", "U", 16384) for i in range(2)]
    u_w2e = [at(f"u_w2e{i}", [128, 8, D], BF16, ubase, 32768 + 16384 * i,
                f"w2e{i}", "U", 16384) for i in range(2)]
    u_hF = at("u_hF", [128, 4, TAU], BF16, ubase, 65536, None, "U", 0)
    for i in range(4):
        p.reg(f"hF{i}", "U", 65536 + 1024 * i, 65536 + 1024 * (i + 1))
    u_att = at("u_att", [128, 8, T], BF16, ubase, 0, "att", "U", 32768)
    u_qh = at("u_qh", [64, T], BF16, ubase, 32768, "qh", "U", 4096)
    u_kh = at("u_kh", [64, T], BF16, ubase, 36864, "kh", "U", 4096)
    u_vh = at("u_vh", [64, T], BF16, ubase, 40960, "vh", "U", 4096)
    u_qf = at("u_qf", [128, 2, T], BF16, ubase, 45056, "qf", "U", 8192)
    u_kfm = at("u_kfm", [128, 2, T], BF16, ubase, 53248, "kfm", "U", 8192)
    u_kft = at("u_kft", [128, 16, 128], BF16, ubase, 32768, "kft", "U", 4096)
    u_attst = at("u_attst", [64, T], BF16, ubase, 36864, "attst", "U", 4096)
    u_vt = at("u_vt", [128, 16, 65], BF16, ubase, 61440, "vt", "U", 2080)
    p.reg("vt_ones", "U", 61440, 61440 + 2080)
    u_qkvw = [at(f"u_qkvw{i}", [128, 8, 192], BF16, ubase, 63552 + 3072 * i,
                 f"qkvw{i}", "U", 3072) for i in range(2)]
    u_y3 = at("u_y3", [128, 8, T], BF16, ubase, 0, None, "U", 0)
    for c in range(NCH):
        p.reg(f"y3{c}", "U", 4096 * c, 4096 * (c + 1))
    p.reg("lnscr", "U", 0, 4096)
    u_acc = at("u_acc", [128, 8, TAU], F32, ubase, 32768, "acc", "U", 16384)
    u_mw1 = [at(f"u_mw1{i}", [128, 8, 512], BF16, ubase, 49152 + 8192 * i,
                f"mw1{i}", "U", 8192) for i in range(2)]
    u_mw2 = at("u_mw2", [128, 4, D], BF16, ubase, 65536, "mw2", "U", 8192)

    a_st = [at(f"a_st{i}", [128, 2, 1024], BF16, abase, 4096 * i,
               f"ast{i}", "A2", 4096) for i in range(2)]
    p.reg("ast0sq", "A2", 2048, 4096)
    p.reg("ast1sq", "A2", 6144, 8192)
    a_rM = at("a_rM", [1, 512], F32, abase, 8192, "rM", "A2", 2048)
    a_rV = at("a_rV", [1, 512], F32, abase, 10240, "rV", "A2", 2048)
    a_rS = at("a_rS", [1, 512], F32, abase, 12288, "rS", "A2", 2048)
    a_rbf = at("a_rbf", [2, T], BF16, abase, 24576, "rbf", "A2", 4096)
    a_r2b = at("a_r2b", [1, T], BF16, abase, 14336, "r2b", "A2", 4096)
    a_gm = at("a_gm", [128, 16, TAU], BF16, abase, 0, None, "A2", 0)
    for i in range(16):
        p.reg(f"gm{i}", "A2", 1024 * i, 1024 * (i + 1))
    a_lg = at("a_lg", [9, T], F32, abase, 8192, "lg", "A2", 8192)
    a_ltok = at("a_ltok", [128, 144], F32, abase, 16384, "ltok", "A2", 576)
    a_m8 = at("a_m8", [128, 128], F32, abase, 16960, "m8", "A2", 512)
    a_sel = at("a_sel", [128, 128], F32, abase, 17472, "sel", "A2", 512)
    a_wrk = at("a_wrk", [128, 128], F32, abase, 17984, "wrk", "A2", 512)
    a_nm = at("a_nm", [128, 16], F32, abase, 18496, "nm", "A2", 64)
    a_dn = at("a_dn", [128, 16], F32, abase, 18560, "dn", "A2", 64)
    a_rc = at("a_rc", [128, 16], F32, abase, 18624, "rc", "A2", 64)
    a_mu = at("a_mu", [1, T], BF16, abase, 19968, "mu", "A2", 4096)
    a_mrow = at("a_mrow", [8, T], BF16, abase, 19456, "mrow", "A2", 4096)
    a_mb = [at(f"a_mb{i}", [128, T], BF16, abase, 23552 + 4096 * i,
               f"mb{i}", "A2", 4096) for i in range(2)]
    a_elu = [at(f"a_elu{i}", [128, TAU], BF16, abase, 1024 * i,
                f"elu{i}", "A2", 1024) for i in range(2)]
    a_elu2 = [at(f"a_elu2{i}", [128, TAU], BF16, abase, 7168 + 1024 * i,
                 f"elu2{i}", "A2", 1024) for i in range(2)]
    a_zb = at("a_zb", [64, TAU], BF16, abase, 2048, "zb", "A2", 1024)
    a_z = at("a_z", [1, T], BF16, abase, 3072, "z", "A2", 4096)
    a_gz = at("a_gz", [2, 2 * D], BF16, abase, 18432, "gz", "A2", 4096)
    a_beta = at("a_beta", [1, D], BF16, abase, 22528, "beta", "A2", 2048)
    a_wout = at("a_wout", [128, 8, D], BF16, abase, 14336, "wout", "A2", 16384)
    a_kvks = at("a_kvks", [128, 2, 65], BF16, abase, 31744, "kvks", "A2", 260)
    a_hM = at("a_hM", [128, 4, TAU], BF16, abase, 0, None, "A2", 0)
    for i in range(4):
        p.reg(f"hM{i}", "A2", 1024 * i, 1024 * (i + 1))
    a_hM2 = at("a_hM2", [128, 4, TAU], BF16, abase, 4096, None, "A2", 0)
    for i in range(4):
        p.reg(f"hN{i}", "A2", 4096 + 1024 * i, 4096 + 1024 * (i + 1))
    a_ph = [at(f"a_ph{i}", [64, FP], BF16, abase, 30720 + 512 * i,
               f"ph{i}", "A2", 512) for i in range(2)]

    s_id32 = nc.alloc_sbuf_tensor("s_id32", [128, 128], F32)
    s_idbf = nc.alloc_sbuf_tensor("s_idbf", [128, 128], BF16)
    s_ones = nc.alloc_sbuf_tensor("s_ones", [1, 512], BF16)
    s_onec = nc.alloc_sbuf_tensor("s_onec", [128, 1], BF16)
    s_sel = nc.alloc_sbuf_tensor("s_sel", [2, 256], BF16)
    s_oneh = nc.alloc_sbuf_tensor("s_oneh", [8, D], BF16)
    s_wro = nc.alloc_sbuf_tensor("s_wro", [128, 8, 8], BF16)
    s_wro32 = nc.alloc_sbuf_tensor("s_wro32", [128, 8, 8], F32)
    s_onec32 = nc.alloc_sbuf_tensor("s_onec32", [128, 1], F32)
    s_csrb = nc.alloc_sbuf_tensor("s_csrb", [1, 8], BF16)
    s_bout = nc.alloc_sbuf_tensor("s_bout", [128, 8], F32)
    s_b2 = nc.alloc_sbuf_tensor("s_b2", [128, 8], F32)
    s_b1 = nc.alloc_sbuf_tensor("s_b1", [128, 32], F32)
    s_cst = nc.alloc_sbuf_tensor("s_cst", [128, 8], F32)

    PS = [nc.alloc_psum_tensor(f"PS{i}", [128, 512], F32) for i in range(7)]
    PT = nc.alloc_psum_tensor("PT", [128, 1024], BF16)

    C_EPS, C_NHALF, C_SQH, C_NEG1 = 0, 1, 2, 3

    def ffn_load_e(e):
        s = e % 2
        p.dma("gdma", (lambda en, e=e, s=s: en.dma_start(
            out=u_w1e[s][:], in_=d_w1all[e].rearrange("(k p) m -> p k m", p=128))),
            writes=(f"w1e{s}",))
        p.dma("gdma", (lambda en, e=e, s=s: en.dma_start(
            out=u_w2e[s][:], in_=d_w2all[e].rearrange("(k p) m -> p k m", p=128))),
            writes=(f"w2e{s}",))

    for c in range(NCH):
        p.dma("sdma", (lambda e, c=c: e.dma_start(
            out=X[c][:], in_=d_x[c * 128:(c + 1) * 128, :])), writes=(f"X{c}",))
    p.dma("gdma", lambda e: e.dma_start(out=s_id32[:], in_=d_idf), writes=("id32",))
    p.dma("gdma", lambda e: e.dma_start(out=s_idbf[:], in_=d_idf), writes=("idbf",))
    p.dma("gdma", lambda e: e.dma_start(out=s_sel[:], in_=d_sel), writes=("selr",))
    p.dma("gdma", lambda e: e.dma_start(out=s_oneh[:], in_=d_oneh), writes=("oneh",))
    p.dma("gdma", lambda e: e.dma_start(
        out=s_wro[:], in_=d_wrT.rearrange("(k p) m -> p k m", p=128)), writes=("wro",))
    p.dma("gdma", lambda e: e.dma_start(
        out=s_wro32[:], in_=d_wrT.rearrange("(k p) m -> p k m", p=128)),
        writes=("wro32",))
    p.dma("gdma", lambda e: e.dma_start(out=s_bout[:], in_=d_boutc), writes=("bout",))
    p.dma("gdma", lambda e: e.dma_start(out=s_b2[:], in_=d_b2c), writes=("b2",))
    p.dma("gdma", lambda e: e.dma_start(out=s_b1[:], in_=d_b1c), writes=("b1",))
    p.dma("gdma", lambda e: e.dma_start(out=s_cst[:], in_=d_csts), writes=("cst",))
    ffn_load_e(0)
    ffn_load_e(1)
    p.op("vector", lambda e: e.memset(s_ones[:], 1.0), writes=("onesr",))
    p.op("vector", lambda e: e.memset(s_onec[:], 1.0), writes=("onec",))
    p.op("vector", lambda e: e.memset(s_onec32[:], 1.0), writes=("onec32",))

    psrot = [0]

    def take_ps():
        i = psrot[0] % 7
        psrot[0] += 1
        return PS[i], f"PS{i}"

    def emit_ln(src_aps, src_keys, src_is_bf, gamma_beta, dst_aps, dst_keys,
                add_into_x=False, dbg_rows=False):
        for half in range(2):
            h0 = half * 1024
            pstiles = [take_ps() for _ in range(4)]
            for c in range(NCH):
                st = a_st[c % 2]
                stk = f"ast{c % 2}"
                sqk = f"ast{c % 2}sq"
                if src_is_bf:
                    xb_ap = src_aps[c][:, h0:h0 + 1024]
                    xb_key = src_keys[c]
                else:
                    p.op("vector",
                         (lambda e, c=c, h0=h0, st=st: e.tensor_copy(
                             st[:, 0, :], src_aps[c][:, h0:h0 + 1024])),
                         reads=(src_keys[c],), writes=(stk,))
                    xb_ap = st[:, 0, :]
                    xb_key = stk
                p.op("vector",
                     (lambda e, st=st, xb_ap=xb_ap: e.tensor_tensor(
                         st[:, 1, :], xb_ap, xb_ap, ALU.mult)),
                     reads=(xb_key,), writes=(sqk,))
                for tt in range(2):
                    for which in range(2):
                        pst, pstk = pstiles[2 * which + tt]
                        rhs = xb_ap[:, tt * 512:(tt + 1) * 512] if which == 0 \
                            else st[:, 1, tt * 512:(tt + 1) * 512]
                        rk = xb_key if which == 0 else sqk
                        p.op("tensor",
                             (lambda e, pst=pst, rhs=rhs, c=c: e.matmul(
                                 pst[0:1, :], s_onec[:], rhs,
                                 start=(c == 0), stop=(c == NCH - 1))),
                             reads=(rk, "onec"), writes=(pstk,))
            inv_n = 1.0 / D
            for tt in range(2):
                col = h0 + tt * 512
                ps1, k1 = pstiles[tt]
                ps2, k2 = pstiles[2 + tt]
                p.op("vector", (lambda e, ps1=ps1: e.tensor_scalar_mul(
                    a_rM[0:1, :], ps1[0:1, :], inv_n)), reads=(k1,), writes=("rM",))
                p.op("vector", (lambda e, ps2=ps2: e.tensor_scalar_mul(
                    a_rV[0:1, :], ps2[0:1, :], inv_n)), reads=(k2,), writes=("rV",))
                if dbg_rows and half == 0 and tt == 0:
                    p.dma("gdma", (lambda e: e.dma_start(
                        out=d_out[7:8, 0:512], in_=a_rV[0:1, :])),
                        reads=("rV",), writes=("dbg3",))
                    p.op("vector", (lambda e, ps2=ps2: e.tensor_copy(
                        a_wrk[:, 0:4], ps2[0:128, 0:4])), reads=(k2,),
                        writes=("wrk",))
                    p.dma("gdma", (lambda e: e.dma_start(
                        out=d_out[9:10, 0:4], in_=a_wrk[0:1, 0:4])),
                        reads=("wrk",), writes=("dbg5",))
                    p.dma("gdma", (lambda e: e.dma_start(
                        out=d_out[10:138, 0:1024],
                        in_=a_st[1][:, 1, :])),
                        reads=("ast1sq",), writes=("dbg6",))
                    p.dma("gdma", (lambda e: e.dma_start(
                        out=d_out[138:266, 0:1024],
                        in_=a_st[0][:, 1, :])),
                        reads=("ast0sq",), writes=("dbg7",))
                p.op("scalar", (lambda e: e.activation(
                    a_rS[0:1, :], a_rM[0:1, :], AF.Square)),
                    reads=("rM",), writes=("rS",))
                p.op("vector", (lambda e: e.tensor_tensor(
                    a_rV[0:1, :], a_rV[0:1, :], a_rS[0:1, :], ALU.subtract)),
                    reads=("rV", "rS"), writes=("rV",))
                if dbg_rows and half == 0 and tt == 0:
                    p.dma("gdma", (lambda e: e.dma_start(
                        out=d_out[8:9, 0:512], in_=a_rV[0:1, :])),
                        reads=("rV",), writes=("dbg4",))
                p.op("scalar", (lambda e: e.activation(
                    a_rV[0:1, :], a_rV[0:1, :], AF.Ln,
                    bias=s_cst[0:1, C_EPS:C_EPS + 1])),
                    reads=("rV", "cst"), writes=("rV",))
                p.op("scalar", (lambda e, col=col: e.activation(
                    a_rbf[0:1, col:col + 512], a_rV[0:1, :], AF.Exp,
                    scale=s_cst[0:1, C_NHALF:C_NHALF + 1])),
                    reads=("rV", "cst"), writes=("rbf",))
                p.op("vector", (lambda e, col=col: e.scalar_tensor_tensor(
                    a_r2b[0:1, col:col + 512], a_rM[0:1, :], -1.0,
                    a_rbf[0:1, col:col + 512], ALU.mult, ALU.mult)),
                    reads=("rM", "rbf"), writes=("r2b",))
                if not gamma_beta:
                    p.op("vector", (lambda e, col=col: e.tensor_copy(
                        a_mu[0:1, col:col + 512], a_rM[0:1, :])),
                        reads=("rM",), writes=("mu",))
                if dbg_rows and half == 0 and tt == 0:
                    p.dma("gdma", (lambda e: e.dma_start(
                        out=d_out[4:5, 0:512], in_=a_rM[0:1, :])),
                        reads=("rM",), writes=("dbg0",))
                    p.dma("gdma", (lambda e: e.dma_start(
                        out=d_out[5:6, 0:512], in_=a_rV[0:1, :])),
                        reads=("rV",), writes=("dbg1",))
                    p.dma("gdma", (lambda e: e.dma_start(
                        out=d_out[6:7, 0:512], in_=a_rS[0:1, :])),
                        reads=("rS",), writes=("dbg2",))
        p.dma("gdma", lambda e: e.dma_start(out=a_rbf[1:2, :], in_=a_r2b[0:1, :]),
              reads=("r2b",), writes=("rbf",))
        for c in range(NCH):
            for t in range(NT):
                tsl = slice(t * TAU, (t + 1) * TAU)
                psA, kA = take_ps()
                psB, kB = take_ps()
                if gamma_beta:
                    p.op("tensor", (lambda e, psA=psA, tsl=tsl, c=c: e.matmul(
                        psA[:], a_gz[:, c * 128:(c + 1) * 128],
                        a_rbf[:, tsl], start=True, stop=True)),
                        reads=("gz", "rbf"), writes=(kA,))
                    p.op("tensor", (lambda e, psB=psB, tsl=tsl, c=c: e.matmul(
                        psB[:], a_gz[:, D + c * 128:D + (c + 1) * 128],
                        a_rbf[:, tsl], start=True, stop=False)),
                        reads=("gz", "rbf"), writes=(kB,))
                    p.op("tensor", (lambda e, psB=psB, c=c: e.matmul(
                        psB[:], a_beta[0:1, c * 128:(c + 1) * 128],
                        s_ones[0:1, :], start=False, stop=True)),
                        reads=("beta", "onesr"), writes=(kB,))
                else:
                    p.op("tensor", (lambda e, psA=psA, tsl=tsl: e.matmul(
                        psA[:], s_sel[:, 0:128], a_rbf[:, tsl],
                        start=True, stop=True)),
                        reads=("selr", "rbf"), writes=(kA,))
                    p.op("tensor", (lambda e, psB=psB, tsl=tsl: e.matmul(
                        psB[:], s_sel[:, 128:256], a_rbf[:, tsl],
                        start=True, stop=True)),
                        reads=("selr", "rbf"), writes=(kB,))
                if not add_into_x:
                    p.op("vector", (lambda e, c=c, tsl=tsl, psA=psA: e.tensor_tensor(
                        dst_aps[c][:, tsl], src_aps[c][:, tsl], psA[:], ALU.mult)),
                        reads=(src_keys[c], kA), writes=(dst_keys[c],))
                    p.op("vector", (lambda e, c=c, tsl=tsl, psB=psB: e.tensor_tensor(
                        dst_aps[c][:, tsl], dst_aps[c][:, tsl], psB[:], ALU.add)),
                        reads=(dst_keys[c], kB), writes=(dst_keys[c],))
                else:
                    p.op("vector", (lambda e, c=c, tsl=tsl, psA=psA: e.tensor_tensor(
                        dst_aps[0][:, tsl], src_aps[c][:, tsl], psA[:], ALU.mult)),
                        reads=(src_keys[c], kA), writes=(dst_keys[0],))
                    p.op("vector", (lambda e, tsl=tsl, psB=psB: e.tensor_tensor(
                        dst_aps[0][:, tsl], dst_aps[0][:, tsl], psB[:], ALU.add)),
                        reads=(dst_keys[0], kB), writes=(dst_keys[0],))
                    p.op("vector", (lambda e, c=c, tsl=tsl: e.tensor_tensor(
                        X[c][:, tsl], X[c][:, tsl], dst_aps[0][:, tsl], ALU.add)),
                        reads=(f"X{c}", dst_keys[0]), writes=(f"X{c}",))

    def emit_routing():
        psc, kc = take_ps()
        for c in range(NCH):
            p.op("tensor", (lambda e, psc=psc, c=c: e.matmul(
                psc[0:1, 0:8], s_onec32[:], s_wro32[:, c, :],
                start=(c == 0), stop=(c == NCH - 1))),
                reads=("onec32", "wro32"), writes=(kc,))
        p.op("vector", (lambda e, psc=psc: e.tensor_copy(
            s_csrb[:], psc[0:1, 0:8])), reads=(kc,), writes=("csr",))
        for t in range(NT):
            tsl = slice(t * TAU, (t + 1) * TAU)
            ps, k = take_ps()
            for c in range(NCH):
                p.op("tensor", (lambda e, ps=ps, c=c, tsl=tsl: e.matmul(
                    ps[0:8, :], s_wro32[:, c, :], X[c][:, tsl],
                    start=(c == 0), stop=(c == NCH - 1))),
                    reads=("wro32", f"X{c}"), writes=(k,))
            p.op("vector", (lambda e, ps=ps, tsl=tsl: e.tensor_copy(
                a_lg[0:8, tsl], ps[0:8, :])), reads=(k,), writes=("lg",))
            ps2, k2 = take_ps()
            p.op("tensor", (lambda e, ps2=ps2, tsl=tsl: e.matmul(
                ps2[0:8, :], s_csrb[:], a_mu[0:1, tsl],
                start=True, stop=True)), reads=("csr", "mu"), writes=(k2,))
            p.op("vector", (lambda e, ps2=ps2, tsl=tsl: e.tensor_tensor(
                a_lg[0:8, tsl], a_lg[0:8, tsl], ps2[0:8, :], ALU.subtract)),
                reads=("lg", k2), writes=("lg",))
        p.dma("gdma", lambda e: e.dma_start(
            out=a_lg[8:9, :], in_=a_rbf[0:1, :]),
            reads=("rbf",), writes=("lg",))
        ps, k = take_ps()
        for j in range(16):
            p.op("tensor", (lambda e, ps=ps, j=j: e.transpose(
                ps[:, 9 * j:9 * j + 9], a_lg[:, 128 * j:128 * (j + 1)],
                s_id32[0:9, 0:9])), reads=("lg", "id32"), writes=(k,))
        p.op("vector", (lambda e, ps=ps: e.tensor_copy(a_ltok[:], ps[:, 0:144])),
             reads=(k,), writes=("ltok",))
        for j in range(16):
            js = slice(9 * j, 9 * j + 8)
            rsl = slice(9 * j + 8, 9 * j + 9)
            p.op("vector", (lambda e, js=js: e.max(out=a_m8[:, 8 * 0:8]
                 if False else a_m8[:, js.start // 9 * 8:js.start // 9 * 8 + 8],
                 in_=a_ltok[:, js])),
                 reads=("ltok",), writes=("m8",), sync_out=True)
            m8s = slice(8 * j, 8 * j + 8)
            p.op("vector", (lambda e, j=j, rsl=rsl: e.scalar_tensor_tensor(
                a_nm[:, j:j + 1], a_m8[:, 8 * j:8 * j + 1], -1.0,
                a_ltok[:, rsl], ALU.mult, ALU.mult)),
                reads=("m8", "ltok"), writes=("nm",))
            p.op("vector", (lambda e, j=j, js=js, m8s=m8s: e.tensor_single_scalar(
                a_sel[:, m8s], a_ltok[:, js], a_m8[:, 8 * j + 1:8 * j + 2],
                ALU.is_ge)),
                reads=("ltok",), writes=("sel",), scalar_reads=("m8",))
            p.op("scalar", (lambda e, j=j, js=js, m8s=m8s, rsl=rsl: e.activation(
                a_wrk[:, m8s], a_ltok[:, js], AF.Exp, bias=a_nm[:, j:j + 1],
                scale=a_ltok[:, rsl])),
                reads=("ltok", "nm"), writes=("wrk",))
            p.op("vector", (lambda e, m8s=m8s: e.tensor_tensor(
                a_wrk[:, m8s], a_wrk[:, m8s], a_sel[:, m8s], ALU.mult)),
                reads=("wrk", "sel"), writes=("wrk",))
            p.op("vector", (lambda e, j=j, m8s=m8s: e.reduce_sum(
                a_dn[:, j:j + 1], a_wrk[:, m8s], axis=AXX)),
                reads=("wrk",), writes=("dn",), sync_out=True)
            p.op("vector", (lambda e, j=j: e.reciprocal(
                a_rc[:, j:j + 1], a_dn[:, j:j + 1])),
                reads=("dn",), writes=("rc",), sync_out=True)
            p.op("vector", (lambda e, j=j, m8s=m8s: e.tensor_scalar(
                a_wrk[:, m8s], a_wrk[:, m8s], a_rc[:, j:j + 1], 0.5,
                ALU.mult, ALU.mult)),
                reads=("wrk",), writes=("wrk",), scalar_reads=("rc",))
        for q in range(4):
            ps2, k2 = take_ps()
            for jj in range(4):
                j = 4 * q + jj
                p.op("tensor", (lambda e, ps2=ps2, j=j, jj=jj: e.transpose(
                    ps2[0:8, 128 * jj:128 * (jj + 1)],
                    a_wrk[:, 8 * j:8 * j + 8], s_id32[:])),
                    reads=("wrk", "id32"), writes=(k2,))
            p.op("vector", (lambda e, ps2=ps2, q=q: e.tensor_copy(
                a_mrow[:, 512 * q:512 * (q + 1)], ps2[0:8, 0:512])),
                reads=(k2,), writes=("mrow",))

    def emit_ffn(next_phase_loads=None, pairs=(0, 1, 2, 3, 4)):
        emit_routing()
        for pair in pairs:
            e0, e1 = 2 * pair, 2 * pair + 1
            for sub, e in ((0, e0), (1, e1)):
                if e < 8:
                    for t in range(NT):
                        tsl = slice(t * TAU, (t + 1) * TAU)
                        ps, k = take_ps()
                        p.op("tensor", (lambda en, ps=ps, e=e, tsl=tsl: en.matmul(
                            ps[:], s_oneh[:, 128 * e:128 * (e + 1)],
                            a_mrow[:, tsl], start=True, stop=True)),
                            reads=("oneh", "mrow"), writes=(k,))
                        p.op("vector", (lambda en, ps=ps, sub=sub, tsl=tsl:
                                        en.tensor_copy(a_mb[sub][:, tsl], ps[:])),
                             reads=(k,), writes=(f"mb{sub}",))
            for t in range(NT):
                tsl = slice(t * TAU, (t + 1) * TAU)
                for sub, e in ((0, e0), (1, e1)):
                    b = e % 2
                    for fc in range(8):
                        gmk = f"gm{sub * 8 + fc}"
                        hslot = fc % 4
                        hk = f"hF{hslot}"
                        ps, k = take_ps()
                        for kk in range(NCH):
                            p.op("tensor", (lambda en, ps=ps, b=b, kk=kk, fc=fc,
                                            tsl=tsl: en.matmul(
                                ps[:], u_w1e[b][:, kk, fc * 128:(fc + 1) * 128],
                                Y[kk][:, tsl], start=(kk == 0), stop=(kk == NCH - 1))),
                                reads=(f"w1e{b}", f"Y{kk}"), writes=(k,))
                        if e < 8:
                            p.op("scalar", (lambda en, ps=ps, hslot=hslot: en.activation(
                                u_hF[:, hslot, :], ps[:], AF.Relu)),
                                reads=(k,), writes=(hk,))
                            p.op("vector", (lambda en, sub=sub, fc=fc, hslot=hslot,
                                            tsl=tsl: en.tensor_tensor(
                                a_gm[:, sub * 8 + fc, :], u_hF[:, hslot, :],
                                a_mb[sub][:, tsl], ALU.mult)),
                                reads=(hk, f"mb{sub}"), writes=(gmk,))
                            p.op("vector", (lambda en, sub=sub, fc=fc, hslot=hslot:
                                            en.tensor_tensor(
                                a_gm[:, sub * 8 + fc, :], u_hF[:, hslot, :],
                                a_gm[:, sub * 8 + fc, :], ALU.mult)),
                                reads=(hk, gmk), writes=(gmk,))
                        else:
                            p.op("scalar", (lambda en, ps=ps, hslot=hslot: en.activation(
                                u_hF[:, hslot, :], ps[:], AF.Relu,
                                scale=s_cst[:, C_SQH:C_SQH + 1])),
                                reads=(k, "cst"), writes=(hk,))
                            p.op("vector", (lambda en, sub=sub, fc=fc, hslot=hslot:
                                            en.tensor_tensor(
                                a_gm[:, sub * 8 + fc, :], u_hF[:, hslot, :],
                                u_hF[:, hslot, :], ALU.mult)),
                                reads=(hk,), writes=(gmk,))
                for c in range(NCH):
                    ps, k = take_ps()
                    first = True
                    for sub, e in ((0, e0), (1, e1)):
                        b = e % 2
                        for fc in range(8):
                            gmk = f"gm{sub * 8 + fc}"
                            p.op("tensor", (lambda en, ps=ps, b=b, fc=fc, c=c, sub=sub,
                                            first=first: en.matmul(
                                ps[:], u_w2e[b][:, fc, c * 128:(c + 1) * 128],
                                a_gm[:, sub * 8 + fc, :],
                                start=first, stop=(sub == 1 and fc == 7))),
                                reads=(f"w2e{b}", gmk), writes=(k,))
                            first = False
                    p.op("vector", (lambda en, ps=ps, c=c, tsl=tsl: en.tensor_tensor(
                        X[c][:, tsl], X[c][:, tsl], ps[:], ALU.add)),
                        reads=(f"X{c}", k), writes=(f"X{c}",))
            if pair < 4 and (pair + 1) in pairs:
                ffn_load_e(e0 + 2)
                ffn_load_e(e1 + 2)
            elif next_phase_loads is not None:
                next_phase_loads()

    def emit_attn(astop=None):
        p.op("vector", lambda e: e.memset(u_vt[:, :, 64:65], 1.0),
             writes=("vt_ones",))
        p.dma("gdma", lambda e: e.dma_start(
            out=a_wout[:], in_=d_woutT.rearrange("(k p) m -> p k m", p=128)),
            writes=("wout",))
        p.dma("gdma", lambda e: e.dma_start(
            out=u_qkvw[0][:], in_=d_wqkvh[0].rearrange("(k p) m -> p k m", p=128)),
            writes=("qkvw0",))
        p.dma("gdma", lambda e: e.dma_start(out=a_ph[0][:], in_=d_proj[0]),
              writes=("ph0",))
        for h in range(H):
            wb = h % 2
            for part, (dst, dk) in enumerate(((u_qh, "qh"), (u_kh, "kh"),
                                              (u_vh, "vh"))):
                for t in range(NT):
                    tsl = slice(t * TAU, (t + 1) * TAU)
                    ps, k = take_ps()
                    for kk in range(NCH):
                        p.op("tensor", (lambda e, ps=ps, wb=wb, kk=kk, part=part,
                                        tsl=tsl: e.matmul(
                            ps[0:64, :], u_qkvw[wb][:, kk, part * 64:(part + 1) * 64],
                            Y[kk][:, tsl], start=(kk == 0), stop=(kk == NCH - 1))),
                            reads=(f"qkvw{wb}", f"Y{kk}"), writes=(k,))
                    p.op("scalar", (lambda e, ps=ps, dst=dst, tsl=tsl: e.copy(
                        dst[:, tsl], ps[0:64, :])), reads=(k,), writes=(dk,))
            pp = 0
            for (srcap, srck, dstap, dstk) in ((u_qh, "qh", u_qf, "qf"),
                                               (u_kh, "kh", u_kfm, "kfm")):
                for mp in range(2):
                    for t in range(NT):
                        tsl = slice(t * TAU, (t + 1) * TAU)
                        ps, k = take_ps()
                        p.op("tensor", (lambda e, ps=ps, wb=wb, mp=mp, srcap=srcap,
                                        srck=srck, tsl=tsl: e.matmul(
                            ps[:], a_ph[wb][:, mp * 128:(mp + 1) * 128],
                            srcap[:, tsl], start=True, stop=True)),
                            reads=(f"ph{wb}", srck), writes=(k,))
                        if pp % 2 == 0:
                            e0, e1 = a_elu[0], a_elu[1]
                            e0k, e1k = "elu0", "elu1"
                            p.op("scalar", (lambda e, ps=ps, e0=e0: e.activation(
                                e0[:], ps[:], AF.Relu,
                                scale=s_cst[:, C_NEG1:C_NEG1 + 1])),
                                reads=(k, "cst"), writes=(e0k,))
                            p.op("scalar", (lambda e, e0=e0, e1=e1: e.activation(
                                e1[:], e0[:], AF.Exp,
                                scale=s_cst[:, C_NEG1:C_NEG1 + 1])),
                                reads=(e0k, "cst"), writes=(e1k,))
                        else:
                            e0, e1 = a_elu2[0], a_elu2[1]
                            e0k, e1k = "elu20", "elu21"
                            p.op("vector", (lambda e, ps=ps, e0=e0:
                                            e.tensor_single_scalar(
                                e0[:], ps[:], 0.0, ALU.min)),
                                reads=(k,), writes=(e0k,))
                            p.op("scalar", (lambda e, e0=e0, e1=e1: e.activation(
                                e1[:], e0[:], AF.Exp)),
                                reads=(e0k,), writes=(e1k,))
                        p.op("vector", (lambda e, ps=ps, dstap=dstap, mp=mp,
                                        tsl=tsl, e1=e1, e1k=e1k:
                                        e.scalar_tensor_tensor(
                            dstap[:, mp, tsl], ps[:], 0.0, e1[:],
                            ALU.max, ALU.add)),
                            reads=(k, e1k), writes=(dstk,))
                        pp += 1
            if h + 1 < H:
                nb = (h + 1) % 2
                p.dma("gdma", (lambda e, h=h, nb=nb: e.dma_start(
                    out=u_qkvw[nb][:],
                    in_=d_wqkvh[h + 1].rearrange("(k p) m -> p k m", p=128))),
                    writes=(f"qkvw{nb}",))
                p.dma("gdma", (lambda e, h=h, nb=nb: e.dma_start(
                    out=a_ph[nb][:], in_=d_proj[h + 1])), writes=(f"ph{nb}",))
            for half in range(2):
                for tt8 in range(8):
                    tt = half * 8 + tt8
                    p.op("tensor", (lambda e, tt=tt, tt8=tt8: e.transpose(
                        PT[:, 64 * tt8:64 * (tt8 + 1)],
                        u_vh[:, 128 * tt:128 * (tt + 1)], s_idbf[0:64, 0:64])),
                        reads=("vh", "idbf"), writes=("PT",))
                p.op("scalar", (lambda e, half=half: e.copy(
                    u_vt[:, 8 * half:8 * (half + 1), 0:64],
                    PT[:].rearrange("p (tt m) -> p tt m", m=64)[:, 0:8, :])),
                    reads=("PT",), writes=("vt",))
            for mp in range(2):
                for half in range(2):
                    for tt8 in range(8):
                        tt = half * 8 + tt8
                        p.op("tensor", (lambda e, mp=mp, tt=tt, tt8=tt8: e.transpose(
                            PT[:, 128 * tt8:128 * (tt8 + 1)],
                            u_kfm[:, mp, 128 * tt:128 * (tt + 1)], s_idbf[:])),
                            reads=("kfm", "idbf"), writes=("PT",))
                    p.op("scalar", (lambda e, half=half: e.copy(
                        u_kft[:, 8 * half:8 * (half + 1), :],
                        PT[:].rearrange("p (tt m) -> p tt m", m=128))),
                        reads=("PT",), writes=("kft",))
                ps, k = take_ps()
                for tt in range(16):
                    p.op("tensor", (lambda e, ps=ps, tt=tt: e.matmul(
                        ps[:, 0:65], u_kft[:, tt, :], u_vt[:, tt, :],
                        start=(tt == 0), stop=(tt == 15))),
                        reads=("kft", "vt", "vt_ones"), writes=(k,))
                p.op("vector", (lambda e, ps=ps, mp=mp: e.tensor_copy(
                    a_kvks[:, mp, :], ps[:, 0:65])), reads=(k,), writes=("kvks",))
            for t in range(NT):
                tsl = slice(t * TAU, (t + 1) * TAU)
                psn, kn = take_ps()
                for mp in range(2):
                    p.op("tensor", (lambda e, psn=psn, mp=mp, tsl=tsl: e.matmul(
                        psn[0:65, :], a_kvks[:, mp, :], u_qf[:, mp, tsl],
                        start=(mp == 0), stop=(mp == 1))),
                        reads=("kvks", "qf"), writes=(kn,))
                p.op("vector", (lambda e, psn=psn: e.tensor_scalar_add(
                    a_rS[0:1, :], psn[64:65, :], EPS_ATTN)),
                    reads=(kn,), writes=("rS",))
                p.op("scalar", (lambda e: e.activation(
                    a_rS[0:1, :], a_rS[0:1, :], AF.Ln)),
                    reads=("rS",), writes=("rS",))
                p.op("scalar", (lambda e, tsl=tsl: e.activation(
                    a_z[0:1, tsl], a_rS[0:1, :], AF.Exp,
                    scale=s_cst[0:1, C_NEG1:C_NEG1 + 1])),
                    reads=("rS", "cst"), writes=("z",))
                psz, kz = take_ps()
                p.op("tensor", (lambda e, psz=psz, tsl=tsl: e.matmul(
                    psz[0:64, :], s_ones[0:1, 0:64], a_z[0:1, tsl],
                    start=True, stop=True)), reads=("onesr", "z"), writes=(kz,))
                p.op("scalar", (lambda e, psz=psz: e.copy(
                    a_zb[:], psz[0:64, :])), reads=(kz,), writes=("zb",))
                if h % 2 == 0:
                    p.op("vector", (lambda e, psn=psn, h=h, tsl=tsl: e.tensor_tensor(
                        u_att[0:64, h // 2, tsl], psn[0:64, :], a_zb[:], ALU.mult)),
                        reads=(kn, "zb"), writes=("att",))
                else:
                    p.op("vector", (lambda e, psn=psn, tsl=tsl: e.tensor_tensor(
                        u_attst[:, tsl], psn[0:64, :], a_zb[:], ALU.mult)),
                        reads=(kn, "zb"), writes=("attst",))
            if h % 2 == 1:
                p.dma("gdma", (lambda e, h=h: e.dma_start(
                    out=u_att[64:128, h // 2, :], in_=u_attst[:])),
                    reads=("attst",), writes=("att",))
        if astop == "heads":
            for c in range(NCH):
                p.dma("gdma", (lambda e, c=c: e.dma_start(
                    out=d_out[c * 128:(c + 1) * 128, :], in_=u_att[:, c, :])),
                    reads=("att",), writes=(f"o{c}",))
            return
        for t in range(NT):
            tsl = slice(t * TAU, (t + 1) * TAU)
            for c in range(NCH):
                ps, k = take_ps()
                for kk in range(NCH):
                    p.op("tensor", (lambda e, ps=ps, kk=kk, c=c, tsl=tsl: e.matmul(
                        ps[:], a_wout[:, kk, c * 128:(c + 1) * 128],
                        u_att[:, kk, tsl], start=(kk == 0), stop=(kk == NCH - 1))),
                        reads=("wout", "att"), writes=(k,))
                p.op("vector", (lambda e, ps=ps, c=c, tsl=tsl: e.scalar_tensor_tensor(
                    Y[c][:, tsl], ps[:], s_bout[:, c:c + 1], Y[c][:, tsl],
                    ALU.add, ALU.add)),
                    reads=(k, "bout", f"Y{c}"), writes=(f"Y{c}",))
        p.dma("gdma", lambda e: e.dma_start(out=a_gz[:], in_=d_gz), writes=("gz",))
        p.dma("gdma", lambda e: e.dma_start(out=a_beta[:], in_=d_beta), writes=("beta",))
        if astop == "xattn":
            for c in range(NCH):
                p.dma("gdma", (lambda e, c=c: e.dma_start(
                    out=d_out[c * 128:(c + 1) * 128, :], in_=Y[c][:])),
                    reads=(f"Y{c}",), writes=(f"o{c}",))
            return
        emit_ln([Y[c][:] for c in range(NCH)], [f"Y{c}" for c in range(NCH)],
                True, True, [u_y3[:, c, :] for c in range(NCH)],
                [f"y3{c}" for c in range(NCH)])
        if astop == "y3":
            for c in range(NCH):
                p.dma("gdma", (lambda e, c=c: e.dma_start(
                    out=d_out[c * 128:(c + 1) * 128, :], in_=u_y3[:, c, :])),
                    reads=(f"y3{c}",), writes=(f"o{c}",))
            return
        if astop == "mlp0":
            p.dma("gdma", (lambda e: e.dma_start(
                out=d_out[640:768, 0:2048], in_=u_y3[:, 0, :])),
                reads=("y30",), writes=("oo5b",))
        p.dma("gdma", lambda e: e.dma_start(
            out=u_mw1[0][:],
            in_=d_w1T.rearrange("(k p) m -> p k m", p=128)[:, :, 0:512]),
            writes=("mw10",))
        gi = 0
        for q in range(4):
            qsl = slice(q * TAU, (q + 1) * TAU)
            for g in range(8):
                wb1 = gi % 2
                if gi + 1 < 32:
                    g2 = (gi + 1) % 8
                    p.dma("gdma", (lambda e, g2=g2, nb=(gi + 1) % 2: e.dma_start(
                        out=u_mw1[nb][:],
                        in_=d_w1T.rearrange("(k p) m -> p k m", p=128)
                        [:, :, g2 * 512:(g2 + 1) * 512])),
                        writes=(f"mw1{(gi + 1) % 2}",))
                p.dma("gdma", (lambda e, g=g: e.dma_start(
                    out=u_mw2[:],
                    in_=d_w2T[g * 512:(g + 1) * 512, :]
                    .rearrange("(k p) m -> p k m", p=128))),
                    writes=("mw2",))
                gi += 1
                for fcg in range(4):
                    fc = 4 * g + fcg
                    ps, k = take_ps()
                    for kk in range(NCH):
                        p.op("tensor", (lambda e, ps=ps, wb1=wb1, kk=kk, fcg=fcg,
                                        qsl=qsl: e.matmul(
                            ps[:], u_mw1[wb1][:, kk, fcg * 128:(fcg + 1) * 128],
                            u_y3[:, kk, qsl], start=(kk == 0), stop=(kk == NCH - 1))),
                            reads=(f"mw1{wb1}", f"y3{kk}"), writes=(k,))
                    p.op("scalar", (lambda e, ps=ps, fc=fc, fcg=fcg: e.activation(
                        a_hM[:, fcg, :], ps[:], AF.Gelu, bias=s_b1[:, fc:fc + 1])),
                        reads=(k, "b1"), writes=(f"hM{fcg}",))
                    p.op("vector", (lambda e, fcg=fcg: e.tensor_copy(
                        a_hM2[:, fcg, :], a_hM[:, fcg, :])),
                        reads=(f"hM{fcg}",), writes=(f"hN{fcg}",))
                for c in range(NCH):
                    ps, k = take_ps()
                    for fcg in range(4):
                        p.op("tensor", (lambda e, ps=ps, fcg=fcg, c=c: e.matmul(
                            ps[:], u_mw2[:, fcg, c * 128:(c + 1) * 128],
                            a_hM2[:, fcg, :], start=(fcg == 0), stop=(fcg == 3))),
                            reads=("mw2", f"hN{fcg}"), writes=(k,))
                    if g == 0:
                        p.op("vector", (lambda e, ps=ps, c=c, qsl=qsl: e.tensor_tensor(
                            Y[c][:, qsl], u_y3[:, c, qsl], ps[:], ALU.add)),
                            reads=(f"y3{c}", k), writes=(f"Y{c}",))
                    elif g < 7:
                        p.op("vector", (lambda e, ps=ps, c=c, qsl=qsl: e.tensor_tensor(
                            Y[c][:, qsl], Y[c][:, qsl], ps[:], ALU.add)),
                            reads=(f"Y{c}", k), writes=(f"Y{c}",))
                    else:
                        p.op("vector", (lambda e, ps=ps, c=c, qsl=qsl:
                                        e.scalar_tensor_tensor(
                            Y[c][:, qsl], ps[:], s_b2[:, c:c + 1], Y[c][:, qsl],
                            ALU.add, ALU.add)),
                            reads=(k, "b2", f"Y{c}"), writes=(f"Y{c}",))
                if astop == "mlp0" and q == 0 and g == 1:
                    p.dma("gdma", (lambda e: e.dma_start(
                        out=d_out[0:128, 0:2048],
                        in_=a_hM[:].rearrange("p a b -> p (a b)"))),
                        reads=("hM0", "hM1", "hM2", "hM3"), writes=("oo0",))
                    p.dma("gdma", (lambda e: e.dma_start(
                        out=d_out[128:256, 0:2048],
                        in_=u_acc[:, 0:4, :].rearrange("p a b -> p (a b)"))),
                        reads=("acc",), writes=("oo1",))
                    p.dma("gdma", (lambda e: e.dma_start(
                        out=d_out[256:384, 0:512], in_=u_mw1[1][:, 0, :])),
                        reads=("mw11",), writes=("oo2",))
                    p.dma("gdma", (lambda e: e.dma_start(
                        out=d_out[384:512, 0:1024], in_=u_mw2[:, 0, :])),
                        reads=("mw2",), writes=("oo3",))
                    p.dma("gdma", (lambda e: e.dma_start(
                        out=d_out[512:640, 0:2048], in_=u_y3[:, 0, :])),
                        reads=("y30",), writes=("oo4",))
                    return
            if astop == "mlpq":
                p.dma("sdma", (lambda e, q=q: e.dma_start(
                    out=d_out[q * 128:(q + 1) * 128, 0:1024],
                    in_=Y[0][:].bitcast(F32))),
                    reads=("Y0",), writes=(f"oq{q}",))
                p.dma("sdma", (lambda e, q=q: e.dma_start(
                    out=d_out[512 + q * 128:512 + (q + 1) * 128, :],
                    in_=u_acc[:, 0:4, :].rearrange("p a b -> p (a b)"))),
                    reads=("acc",), writes=(f"oq2{q}",))
        if astop == "mlpq":
            return
        if astop == "x2":
            for c in range(NCH):
                p.dma("gdma", (lambda e, c=c: e.dma_start(
                    out=d_out[c * 128:(c + 1) * 128, :], in_=Y[c][:])),
                    reads=(f"Y{c}",), writes=(f"o{c}",))
            return
        emit_ln([Y[c][:] for c in range(NCH)], [f"Y{c}" for c in range(NCH)],
                True, True, [u_y3[:, 0, :]], ["lnscr"], add_into_x=True,
                dbg_rows=(astop == "ln4s"))

    def _mac_ln():
        emit_ln([X[c][:] for c in range(NCH)], [f"X{c}" for c in range(NCH)],
                False, False, [Y[c][:] for c in range(NCH)],
                [f"Y{c}" for c in range(NCH)])

    stages = [
        ("ln_a", _mac_ln),
        ("ffn1", emit_ffn),
        ("ln_b", _mac_ln),
        ("attn", emit_attn),
        ("pre2", lambda: (ffn_load_e(0), ffn_load_e(1))),
        ("ln_c", _mac_ln),
        ("ffn2", emit_ffn),
    ]
    if stop_after == "wro":
        p.op("vector", lambda e: e.tensor_copy(
            a_ltok[:, 0:64], s_wro[:].rearrange("p k m -> p (k m)")),
            reads=("wro", "w1e0", "w1e1", "w2e0", "w2e1", "cst", "b1", "b2",
                   "bout", "oneh", "selr", "idbf", "id32"),
            writes=("ltok",))
        p.dma("gdma", lambda e: e.dma_start(
            out=d_out[0:128, 0:64], in_=a_ltok[:, 0:64]),
            reads=("ltok",), writes=("o0",))
        stages = []
    elif stop_after in ("lg1", "route1", "routedbg"):
        _mac_ln()
        emit_routing()
        if stop_after == "lg1":
            p.dma("gdma", lambda e: e.dma_start(out=d_out[0:8, :], in_=a_lg[:]),
                  reads=("lg",), writes=("o0",))
        elif stop_after == "route1":
            p.dma("gdma", lambda e: e.dma_start(out=d_out[0:8, :], in_=a_mrow[:]),
                  reads=("mrow",), writes=("o0",))
        else:
            for nm, ap_, key, w_ in (("wrk", a_wrk[:], "wrk", 128),
                                     ("dn", a_dn[:], "dn", 16),
                                     ("rc", a_rc[:], "rc", 16),
                                     ("m8", a_m8[:], "m8", 128),
                                     ("sel", a_sel[:], "sel", 128),
                                     ("ltok", a_ltok[:], "ltok", 128),
                                     ("nm", a_nm[:], "nm", 16)):
                pass
            col = [0]

            def dump(ap_, key, w_):
                c0 = col[0]
                p.dma("gdma", (lambda e, ap_=ap_, c0=c0, w_=w_: e.dma_start(
                    out=d_out[0:128, c0:c0 + w_], in_=ap_)),
                    reads=(key,), writes=("o0",))
                col[0] += w_
            dump(a_wrk[:], "wrk", 128)
            dump(a_dn[:], "dn", 16)
            dump(a_rc[:], "rc", 16)
            dump(a_m8[:], "m8", 128)
            dump(a_sel[:], "sel", 128)
            dump(a_ltok[:], "ltok", 128)
            dump(a_nm[:], "nm", 16)
        stages = []
    if stop_after in ("heads", "xattn", "y3", "x2", "mlp0", "mlpq"):
        _mac_ln()
        emit_ffn()
        _mac_ln()
        emit_attn(astop=stop_after)
        stages = []
    if stop_after == "ln4s":
        _mac_ln()
        emit_ffn()
        _mac_ln()
        emit_attn(astop="ln4s")
        p.dma("gdma", (lambda e: e.dma_start(out=d_out[0:2, :], in_=a_rbf[:])),
              reads=("rbf",), writes=("oo0",))
        stages = []
    if stop_after == "ln4r":
        _mac_ln()
        emit_ffn()
        _mac_ln()
        emit_attn()
        p.dma("gdma", (lambda e: e.dma_start(out=d_out[0:2, :], in_=a_rbf[:])),
              reads=("rbf",), writes=("oo0",))
        p.dma("gdma", (lambda e: e.dma_start(out=d_out[2:4, 0:1024],
                                             in_=a_gz[:, 0:1024])),
              reads=("gz",), writes=("oo1",))
        p.dma("sdma", (lambda e: e.dma_start(out=d_out[128:256, :], in_=X[0][:])),
              reads=("X0",), writes=("oo2",))
        stages = []
    if stop_after == "ffnp0d":
        _mac_ln()
        emit_ffn(pairs=(0,))
        def dmp(row, ap_, key):
            p.dma("gdma", (lambda e, ap_=ap_, row=row: e.dma_start(
                out=d_out[row:row + 128, 0:ap_.shape[-1] if False else 2048],
                in_=ap_)), reads=(key,), writes=(f"o{row}",))
        p.dma("gdma", (lambda e: e.dma_start(out=d_out[0:128, :], in_=a_mb[0][:])),
              reads=("mb0",), writes=("oo0",))
        p.dma("gdma", (lambda e: e.dma_start(
            out=d_out[256:384, :], in_=a_gm[:, 0:4, :].rearrange("p a b -> p (a b)"))),
            reads=("gm0", "gm1", "gm2", "gm3"), writes=("oo2",))
        p.dma("gdma", (lambda e: e.dma_start(
            out=d_out[384:512, :], in_=a_gm[:, 4:8, :].rearrange("p a b -> p (a b)"))),
            reads=("gm4", "gm5", "gm6", "gm7"), writes=("oo3",))
        p.dma("sdma", (lambda e: e.dma_start(out=d_out[512:640, :], in_=X[0][:])),
              reads=("X0",), writes=("oo4",))
        p.dma("gdma", (lambda e: e.dma_start(out=d_out[640:648, :], in_=a_mrow[:])),
              reads=("mrow",), writes=("oo5",))
        p.dma("gdma", (lambda e: e.dma_start(
            out=d_out[648:776, :], in_=a_gm[:, 8:12, :].rearrange("p a b -> p (a b)"))),
            reads=("gm8", "gm9", "gm10", "gm11"), writes=("oo6",))
        p.dma("gdma", (lambda e: e.dma_start(
            out=d_out[776:904, :], in_=a_gm[:, 12:16, :].rearrange("p a b -> p (a b)"))),
            reads=("gm12", "gm13", "gm14", "gm15"), writes=("oo7b",))
        p.dma("gdma", (lambda e: e.dma_start(
            out=d_out[128:256, 0:1024], in_=u_w2e[0][:, 0, :])),
            reads=("w2e0",), writes=("oo1",))
        for rep in range(2):
            ps_d, k_d = take_ps()
            first = True
            for sub in (0, 1):
                for fc in range(8):
                    p.op("tensor", (lambda en, ps_d=ps_d, sub=sub, fc=fc,
                                    first=first: en.matmul(
                        ps_d[:], u_w2e[sub][:, fc, 0:128],
                        a_gm[:, sub * 8 + fc, :],
                        start=first, stop=(sub == 1 and fc == 7))),
                        reads=(f"w2e{sub}", f"gm{sub * 8 + fc}"), writes=(k_d,))
                    first = False
            p.op("vector", (lambda e, ps_d=ps_d, rep=rep: e.tensor_copy(
                a_wrk[:, 0:128], ps_d[:, 0:128])), reads=(k_d,), writes=("wrk",))
            p.dma("gdma", (lambda e, rep=rep: e.dma_start(
                out=d_out[896:1024, 128 * rep:128 * (rep + 1)],
                in_=a_wrk[:, 0:128])),
                reads=("wrk",), writes=(f"oo8{rep}",))
        for slot in range(16):
            p.op("vector", (lambda e, slot=slot: e.tensor_copy(
                Y[slot // 4][:, (slot % 4) * 512:(slot % 4 + 1) * 512],
                a_gm[:, slot, :])),
                reads=(f"gm{slot}",), writes=(f"Y{slot // 4}",))
        ps_e, k_e = take_ps()
        first = True
        for sub in (0, 1):
            for fc in range(8):
                slot = sub * 8 + fc
                p.op("tensor", (lambda en, ps_e=ps_e, sub=sub, fc=fc, slot=slot,
                                first=first: en.matmul(
                    ps_e[:], u_w2e[sub][:, fc, 0:128],
                    Y[slot // 4][:, (slot % 4) * 512:(slot % 4 + 1) * 512],
                    start=first, stop=(sub == 1 and fc == 7))),
                    reads=(f"w2e{sub}", f"Y{slot // 4}"), writes=(k_e,))
                first = False
        p.op("vector", (lambda e, ps_e=ps_e: e.tensor_copy(
            a_sel[:, 0:128], ps_e[:, 0:128])), reads=(k_e,), writes=("sel",))
        p.dma("gdma", (lambda e: e.dma_start(
            out=d_out[896:1024, 256:384], in_=a_sel[:, 0:128])),
            reads=("sel",), writes=("oo9",))
        ps_f, k_f = take_ps()
        p.op("tensor", (lambda en, ps_f=ps_f: en.matmul(
            ps_f[:], u_w2e[1][:, 1, 0:128], a_gm[:, 9, :],
            start=True, stop=True)), reads=("w2e1", "gm9"), writes=(k_f,))
        p.op("vector", (lambda e, ps_f=ps_f: e.tensor_copy(
            a_ltok[:, 0:128], ps_f[:, 0:128])), reads=(k_f,), writes=("ltok",))
        p.dma("gdma", (lambda e: e.dma_start(
            out=d_out[896:1024, 384:512], in_=a_ltok[:, 0:128])),
            reads=("ltok",), writes=("ooA",))
        p.dma("gdma", (lambda e: e.dma_start(
            out=d_out[0:128, 1024:2048], in_=u_w2e[1][:, 1, :])),
            reads=("w2e1",), writes=("ooB",))
        stages = []
    if stop_after == "ffninc":
        _mac_ln()
        emit_ffn(pairs=(0,))
        for gl in range(16):
            ps_g, k_g = take_ps()
            first = True
            for s in range(gl + 1):
                sub, fc = s // 8, s % 8
                p.op("tensor", (lambda en, ps_g=ps_g, sub=sub, fc=fc, s=s,
                                first=first, gl=gl: en.matmul(
                    ps_g[:], u_w2e[sub][:, fc, 0:128],
                    a_gm[:, s, :],
                    start=first, stop=(s == gl))),
                    reads=(f"w2e{sub}", f"gm{s}"), writes=(k_g,))
                first = False
            p.op("vector", (lambda e, ps_g=ps_g, gl=gl: e.tensor_copy(
                a_wrk[:, 0:128], ps_g[:, 0:128])), reads=(k_g,), writes=("wrk",))
            p.dma("gdma", (lambda e, gl=gl: e.dma_start(
                out=d_out[(gl // 2) * 128:(gl // 2 + 1) * 128,
                          1536 + (gl % 2) * 128:1536 + (gl % 2 + 1) * 128],
                in_=a_wrk[:, 0:128])),
                reads=("wrk",), writes=(f"oi{gl}",))
        for tag, slots in (("s14", (14,)), ("p1314", (13, 14)), ("s15", (15,))):
            ps_h, k_h = take_ps()
            first = True
            for s in slots:
                sub, fc = s // 8, s % 8
                p.op("tensor", (lambda en, ps_h=ps_h, sub=sub, fc=fc, s=s,
                                first=first, last=(s == slots[-1]): en.matmul(
                    ps_h[:], u_w2e[sub][:, fc, 0:128],
                    a_gm[:, s, :], start=first, stop=last)),
                    reads=(f"w2e{sub}", f"gm{s}"), writes=(k_h,))
                first = False
            p.op("vector", (lambda e, ps_h=ps_h: e.tensor_copy(
                a_wrk[:, 0:128], ps_h[:, 0:128])), reads=(k_h,), writes=("wrk",))
            row = {"s14": 1792, "p1314": 1792 + 0, "s15": 1792}[tag]
            col = {"s14": 1024, "p1314": 1152, "s15": 1280}[tag]
            p.dma("gdma", (lambda e, col=col: e.dma_start(
                out=d_out[896:1024, col:col + 128], in_=a_wrk[:, 0:128])),
                reads=("wrk",), writes=(f"oh{tag}",))
        for s in range(16):
            p.dma("gdma", (lambda e, s=s: e.dma_start(
                out=d_out[(s // 2) * 128:(s // 2 + 1) * 128,
                          (s % 2) * 512:(s % 2 + 1) * 512],
                in_=a_gm[:, s, :])),
                reads=(f"gm{s}",), writes=(f"og{s}",))
        stages = []
    if stop_after in ("ffn1p0", "ffn1sh"):
        _mac_ln()
        if stop_after == "ffn1p0":
            emit_ffn(pairs=(0,))
        else:
            ffn_load_e(8)
            ffn_load_e(9)
            emit_ffn(pairs=(4,))
        for c in range(NCH):
            p.dma("sdma", (lambda e, c=c: e.dma_start(
                out=d_out[c * 128:(c + 1) * 128, :], in_=X[c][:])),
                reads=(f"X{c}",), writes=(f"o{c}",))
        stages = []
    for sname, sfn in stages:
        sfn()
        if stop_after == sname:
            break
    if stop_after in ("wro", "lg1", "route1", "routedbg", "ffn1p0", "ffn1sh",
                      "ffnp0d", "ffninc", "heads", "xattn", "y3", "x2", "mlp0",
                      "mlpq", "ln4r", "ln4s"):
        pass
    elif stop_after in ("ln_a", "ln_b", "ln_c"):
        for c in range(NCH):
            p.dma("gdma", (lambda e, c=c: e.dma_start(
                out=d_out[c * 128:(c + 1) * 128, :], in_=Y[c][:])),
                reads=(f"Y{c}",), writes=(f"o{c}",))
    else:
        for c in range(NCH):
            p.dma("sdma", (lambda e, c=c: e.dma_start(
                out=d_out[c * 128:(c + 1) * 128, :], in_=X[c][:])),
                reads=(f"X{c}",), writes=(f"o{c}",))

    with (
        nc.semaphore("smT") as smT, nc.semaphore("smV") as smV,
        nc.semaphore("smS") as smS, nc.semaphore("smG") as smG,
        nc.semaphore("smY") as smY, nc.semaphore("dmG") as dmG,
        nc.semaphore("dmY") as dmY,
    ):
        sems = {"tensor": smT, "vector": smV, "scalar": smS,
                "gpsimd": smG, "sync": smY}
        dma_sems = {"gdma": dmG, "sdma": dmY}
        allsems = [smT, smV, smS, smG, smY, dmG, dmY]
        nc.all_engine_barrier()
        rng = range(min(s.num for s in allsems), max(s.num for s in allsems) + 1)
        nc.gpsimd.dma_reset(rng)
        nc.gpsimd.sem_clear(rng)
        nc.all_engine_barrier()
        replay = p.finalize(sems, dma_sems)
        n_sdma = p.nidx["sdma"]
        n_gdma = p.nidx["gdma"]
        with nc.Block() as block:
            block.tensor(replay("tensor"))
            block.vector(replay("vector"))
            block.scalar(replay("scalar"))

            def gpsbody(engobj):
                replay("gpsimd")(engobj)
                engobj.wait_ge(dmG, 16 * n_gdma)
            block.gpsimd(gpsbody)

            def syncbody(engobj):
                replay("sync")(engobj)
                engobj.wait_ge(dmY, 16 * n_sdma)
            block.sync(syncbody)
    return nc


def _host_prep(inputs):
    f32 = np.float32
    x = np.asarray(inputs["L0_attn_x"], f32)
    wqkv = np.asarray(inputs["L0_attn_w_qkv"], f32)
    proj = np.asarray(inputs["L0_attn_proj"], f32)
    wout = np.asarray(inputs["L0_attn_w_out"], f32)
    bout = np.asarray(inputs["L0_attn_b_out"], f32)
    gamma = np.asarray(inputs["L0_attn_gamma"], f32)
    beta = np.asarray(inputs["L0_attn_beta"], f32)
    w1 = np.asarray(inputs["L0_attn_w1"], f32)
    b1 = np.asarray(inputs["L0_attn_b1"], f32)
    w2 = np.asarray(inputs["L0_attn_w2"], f32)
    b2 = np.asarray(inputs["L0_attn_b2"], f32)
    wr = np.asarray(inputs["L0_ffn_w_router"], f32)
    w1s = np.asarray(inputs["L0_ffn_w1_stack"], f32)
    w2s = np.asarray(inputs["L0_ffn_w2_stack"], f32)
    ws1 = np.asarray(inputs["L0_ffn_ws1"], f32)
    ws2 = np.asarray(inputs["L0_ffn_ws2"], f32)

    qT = wqkv[0:D].T
    kT = wqkv[D:2 * D].T
    vT = wqkv[2 * D:3 * D].T
    wqkvh = np.empty((H, D, 192), f32)
    for h in range(H):
        wqkvh[h, :, 0:64] = qT[:, h * 64:(h + 1) * 64]
        wqkvh[h, :, 64:128] = kT[:, h * 64:(h + 1) * 64]
        wqkvh[h, :, 128:192] = vT[:, h * 64:(h + 1) * 64]

    w1sT = np.ascontiguousarray(w1s.transpose(0, 2, 1))
    w2sT = np.ascontiguousarray(w2s.transpose(0, 2, 1))
    ws1T = np.ascontiguousarray(ws1.T)
    ws2T = np.ascontiguousarray(ws2.T)
    w1all = np.concatenate(
        [w1sT, ws1T[:, 0:D][None], ws1T[:, D:2 * D][None]], axis=0)
    w2all = np.concatenate(
        [w2sT, ws2T[0:D, :][None], ws2T[D:2 * D, :][None]], axis=0)

    oneh = np.zeros((8, D), f32)
    for e in range(8):
        oneh[e, 128 * e:128 * (e + 1)] = 1.0
    csts = np.zeros((128, 8), f32)
    csts[:, 0] = EPS_LN
    csts[:, 1] = -0.5
    csts[:, 2] = np.sqrt(0.5)
    csts[:, 3] = -1.0
    gz = np.zeros((2, 2 * D), f32)
    gz[0, 0:D] = gamma
    gz[1, D:2 * D] = gamma
    selrow = np.zeros((2, 256), f32)
    selrow[0, 0:128] = 1.0
    selrow[1, 128:256] = 1.0

    common = {
        "wqkvh": wqkvh, "proj": np.ascontiguousarray(proj),
        "woutT": np.ascontiguousarray(wout.T),
        "w1T": np.ascontiguousarray(w1.T),
        "w2T": np.ascontiguousarray(w2.T),
        "wrT": np.ascontiguousarray(wr.T),
        "w1all": w1all, "w2all": w2all,
        "boutc": np.ascontiguousarray(bout.reshape(8, 128).T),
        "b2c": np.ascontiguousarray(b2.reshape(8, 128).T),
        "b1c": np.ascontiguousarray(b1.reshape(32, 128).T),
        "gz": gz, "betar": beta.reshape(1, D), "selrow": selrow,
        "oneh": oneh, "idf": np.eye(128, dtype=f32), "csts": csts,
    }
    in_maps = []
    for b in range(NCORES):
        m = dict(common)
        m["x"] = np.ascontiguousarray(x[b].T)
        in_maps.append(m)
    return in_maps


def kernel(**inputs):
    in_maps = _host_prep(inputs)
    nc = build_program()
    res = run_bass_kernel_spmd(nc, in_maps, core_ids=list(range(NCORES)))
    out = np.stack([np.ascontiguousarray(r["o"].T) for r in res.results], axis=0)
    return out
